# version 19
# baseline (speedup 1.0000x reference)
import contextlib
import math
import numpy as np
import ml_dtypes
import concourse.bass as bass
import concourse.mybir as mybir
from concourse.bass_utils import run_bass_kernel_spmd

F32 = mybir.dt.float32
BF16 = mybir.dt.bfloat16
AF = mybir.ActivationFunctionType
ALU = mybir.AluOpType
AX = mybir.AxisListType

B, L, H = 8, 4096, 1024
NCORE = 8
HSL = H // NCORE
NFFT = 8192
E, INTER = 16, 512

TWO_PI = 2 * math.pi
RC1 = float(np.float32(6.28125))
RC2 = float(np.float32(np.float64(TWO_PI) - 6.28125))
MAGIC = 12582912.0
GELU_C = float(np.sqrt(2.0 / np.pi))
GELU_A = 0.044715

DEBUG = {"taps": {}, "stop_after": None}


def _bfc(x):
    return np.asarray(x, dtype=ml_dtypes.bfloat16)


def host_consts():
    a = np.arange(64, dtype=np.float64)[:, None]
    p = np.arange(128, dtype=np.float64)[None, :]
    F128 = np.exp(-2j * np.pi * a * p / 128.0)
    bb = np.arange(64, dtype=np.float64)[:, None]
    tt = np.arange(64, dtype=np.float64)[None, :]
    W64 = np.exp(-2j * np.pi * bb * tt / 64.0)
    T1 = np.exp(-2j * np.pi * np.arange(64)[:, None] * np.arange(128)[None, :] / NFFT)
    W64c = np.exp(2j * np.pi * np.arange(64)[:, None] * np.arange(64)[None, :] / 64.0)
    T2 = np.exp(2j * np.pi * np.arange(128)[:, None] * np.arange(64)[None, :] / NFFT)
    W128c = np.exp(2j * np.pi * np.arange(128)[:, None] * np.arange(64)[None, :] / 128.0) / NFFT

    c = {}
    cf = np.zeros((128, 256))
    cf[0:64, 0:128] = F128.real
    cf[0:64, 128:256] = F128.imag
    c["cf"] = _bfc(cf)

    def blk(m):
        z = np.zeros((128, 128))
        z[0:64, 0:64] = m
        z[64:128, 64:128] = m
        return _bfc(z)
    c["s2wr"] = blk(W64.real)
    c["s2wi"] = blk(W64.imag)
    c["s2win"] = blk(-W64.imag)
    i1a = np.zeros((128, 256))
    i1a[0:64, 0:64] = W64c.real; i1a[0:64, 64:128] = W64c.imag
    i1a[64:128, 128:192] = W64c.real; i1a[64:128, 192:256] = W64c.imag
    i1b = np.zeros((128, 256))
    i1b[0:64, 0:64] = -W64c.imag; i1b[0:64, 64:128] = W64c.real
    i1b[64:128, 128:192] = -W64c.imag; i1b[64:128, 192:256] = W64c.real
    c["i1a"] = _bfc(i1a)
    c["i1b"] = _bfc(i1b)
    c["i2r"] = _bfc(W128c.real)
    c["i2in"] = _bfc(-W128c.imag)
    c["t1r"] = _bfc(np.vstack([T1.real, T1.real]))
    c["t1i"] = _bfc(np.vstack([T1.imag, T1.imag]))
    c["t2r"] = _bfc(np.hstack([T2.real, T2.real]))
    c["t2i"] = _bfc(np.hstack([T2.imag, T2.imag]))
    c["iota"] = np.broadcast_to(np.arange(64, dtype=np.float32), (64, 64)).copy()
    return c


class Ctx:
    pass


def ap(t, off, dims):
    return bass.AP(t, int(off), [[int(a), int(b)] for a, b in dims])


def build_nc():
    nc = bass.Bass("TRN2", target_bir_lowering=False)
    cx = Ctx()
    cx.nc = nc
    es = contextlib.ExitStack()
    cx.es = es

    I = {}
    def inp(name, shape, dt=F32):
        I[name] = nc.dram_tensor(name, list(shape), dt, kind="ExternalInput")
    inp("x", [L, H])
    inp("logdt_t", [1, H]); inp("arl_t", [64, H]); inp("aim_t", [64, H])
    inp("cr_t", [64, H]); inp("ci_t", [64, H])
    inp("d_t", [128, 8]); inp("wout", [H, 2 * H]); inp("bout_t", [128, 16])
    inp("gw_t", [H, 16]); inp("gb", [1, 16])
    inp("we1", [E, H, INTER]); inp("we2", [E, INTER, H]); inp("we3", [E, H, INTER])
    inp("be1_t", [128, E * 4]); inp("be3_t", [128, E * 4]); inp("be2", [E, H])
    inp("ws1", [H, H]); inp("ws2", [H, H]); inp("ws3", [H, H])
    inp("bs1_t", [128, 8]); inp("bs3_t", [128, 8]); inp("bs2_t", [128, 8])
    hc_consts = host_consts()
    for k, v in hc_consts.items():
        inp("c_" + k, v.shape, BF16 if v.dtype == ml_dtypes.bfloat16 else F32)
    cx.I = I
    cx.out = nc.dram_tensor("out", [L, H], F32, kind="ExternalOutput")
    cx.taps = {nm: nc.dram_tensor("tap_" + nm, list(shape), dt, kind="ExternalOutput")
               for nm, (shape, dt) in DEBUG["taps"].items()}

    D = {}
    D["dA"] = nc.dram_tensor("dA", [L, H], BF16)
    D["dA2"] = nc.dram_tensor("dA2", [L, H], BF16)
    D["dB"] = nc.dram_tensor("dB", [H, L], BF16)
    D["dCMB"] = nc.dram_tensor("dCMB", [L, 128], BF16)
    D["dCMBT"] = nc.dram_tensor("dCMBT", [E, L], BF16)
    D["dKLOC"] = nc.dram_tensor("dKLOC", [H, 64 * 256], BF16)
    D["dWP"] = nc.dram_tensor("dWP", [24 * 64, H], F32)
    D["dY"] = nc.dram_tensor("dY", [H, L], BF16)
    D["dP"] = nc.dram_tensor("dP", [H, L], BF16)
    NIR = E * 4 + 8
    D["dH"] = nc.dram_tensor("dH", [NIR * 128, L], BF16)
    D["dS"] = nc.dram_tensor("dS", [NIR * 128, L], BF16)
    D["dF8"] = nc.dram_tensor("dF8", [64, 8 * H * 8], BF16)
    D["dM"] = nc.dram_tensor("dM", [H, L], BF16)
    D["dDT"] = nc.dram_tensor("dDT", [1, H], F32)
    cx.D = D

    with es:
        cx.S = lambda nm, shp, dt=F32: es.enter_context(nc.sbuf_tensor(nm, shp, dt))
        cx.cst = {k: cx.S("s_" + k, list(v.shape), BF16 if v.dtype == ml_dtypes.bfloat16 else F32)
                  for k, v in hc_consts.items()}
        cx.cmbT = cx.S("cmbT", [128, L], BF16)
        cx.sm = {nm: cx.S("sb_" + nm, [128, w], F32)
                 for nm, w in [("d_t", 8), ("bout_t", 16), ("gb", 16),
                               ("be1_t", E * 4), ("be3_t", E * 4),
                               ("bs1_t", 8), ("bs3_t", 8), ("bs2_t", 8)]}
        cx.PS = es.enter_context(nc.psum_tensor("PS", [128, 4096], F32))
        cx.xnF_cm = nc.sbuf_tensor("xnF", [128, 8 * L], BF16)
        cx.xnF = cx.xnF_cm.__enter__()
        cx.xnF_freed = False

        cx.all_sems = []
        class Sem:
            def __init__(self, name):
                self.h = es.enter_context(nc.semaphore(name))
                self.val = 0
                cx.all_sems.append(self)
        cx.newsem = Sem
        for nm in ["sd", "sg", "sa", "cv", "ca", "ct", "cg", "cc"]:
            setattr(cx, nm, Sem(nm))

        cx.block = es.enter_context(nc.Block())
        phases = ["p0", "p0b", "p1", "p2", "p3", "p4", "p6", "p7"]
        stop = DEBUG["stop_after"]
        emit_init(cx)
        for phname in phases:
            globals()["emit_" + phname](cx)
            if stop == phname:
                break
        emit_taps(cx)
        emit_fin(cx)
        if not cx.xnF_freed:
            cx.xnF_cm.__exit__(None, None, None)
    return nc


def quiesce(cx, eng, snap=None):
    if snap is None:
        snap = [(s, s.val) for s in cx.all_sems]
    for s, v in snap:
        if v > 0:
            eng.wait_ge(s.h, v)


SELF_WAITS = True


def mkvo(cx, v):
    def vo(ins):
        cx.cv.val += 1
        ins.then_inc(cx.cv.h, 1)
    def vw(eng=None):
        if SELF_WAITS or eng is not None:
            (eng or v).wait_ge(cx.cv.h, cx.cv.val)
    return vo, vw


def mkao(cx, s):
    def ao(ins):
        cx.ca.val += 1
        ins.then_inc(cx.ca.h, 1)
    def aw(eng=None):
        if SELF_WAITS or eng is not None:
            (eng or s).wait_ge(cx.ca.h, cx.ca.val)
    return ao, aw


def emit_init(cx):
    nc, I, block = cx.nc, cx.I, cx.block
    sd, cv = cx.sd, cx.cv

    @block.sync
    def _(sync):
        for k, t in cx.cst.items():
            sync.dma_start(out=t[:], in_=I["c_" + k].ap()).then_inc(sd.h, 16)
            sd.val += 16
        for nm, t in cx.sm.items():
            if nm == "gb":
                src = ap(I["gb"], 0, [(0, 128), (1, E)])
            else:
                src = I[nm].ap()
            sync.dma_start(out=t[:], in_=src).then_inc(sd.h, 16)
            sd.val += 16

    @block.vector
    def _(v):
        vo, vw = mkvo(cx, v)
        vw()
        vo(v.memset(cx.cmbT[:], 0.0))
    cx.init_sd = cx.sd.val


def emit_p0(cx):
    nc, block, I = cx.nc, cx.block, cx.I
    sd, sg, cv, ca = cx.sd, cx.sg, cx.cv, cx.ca
    S = cx.S
    NT = 32

    p0 = contextlib.ExitStack()
    with p0:
        S0 = lambda nm, shp, dt=F32: p0.enter_context(nc.sbuf_tensor(nm, shp, dt))
        xt = [S0(f"p0_x{i}", [128, H]) for i in range(3)]
        sq = S0("p0_sq", [128, H])
        xnhi = [S0(f"p0_hi{i}", [128, H], BF16) for i in range(2)]
        xnlo = [S0(f"p0_lo{i}", [128, H], BF16) for i in range(2)]
        xn32 = S0("p0_xn32", [128, H])
        cm = S0("p0_cm", [128, 1]); cq = S0("p0_cq", [128, 1])
        cvv = S0("p0_cv", [128, 1]); rstd = S0("p0_rstd", [128, 1])
        nmr = S0("p0_nmr", [128, 1])

        sdx = [cx.newsem(f"p0x{j}") for j in range(3)]
        sgw = [cx.newsem(f"p0w{j}") for j in range(2)]
        cv0 = cv.val
        ca0 = ca.val
        NV = 9
        NA = 3

        @block.sync
        def _(sync):
            for i in range(NT):
                if i >= 3:
                    sync.wait_ge(cx.ca.h, ca0 + NA * (i - 3) + 1)
                    sync.wait_ge(cx.cv.h, cv0 + NV * (i - 3) + 7)
                sync.dma_start(out=xt[i % 3][:], in_=I["x"].ap()[i * 128:(i + 1) * 128, :]
                               ).then_inc(sdx[i % 3].h, 16)
                sdx[i % 3].val += 16

        @block.vector
        def _(v):
            vo, vw = mkvo(cx, v)
            for i in range(NT):
                xti = xt[i % 3]
                v.wait_ge(sdx[i % 3].h, 16 * (i // 3 + 1))
                vw(); vo(v.reduce_sum(cm[:], xti[:], axis=AX.X))
                v.wait_ge(ca.h, ca0 + NA * i + 1)
                vw(); vo(v.reduce_sum(cq[:], sq[:], axis=AX.X))
                vw(); vo(v.tensor_tensor(cvv[:], cm[:], cm[:], ALU.mult))
                vw(); vo(v.scalar_tensor_tensor(cvv[:], cvv[:], 1.0 / 1024.0, cq[:],
                                                ALU.mult, ALU.subtract))
                vw(); vo(v.tensor_scalar(cvv[:], cvv[:], -1.0 / 1024.0, 1e-5,
                                         ALU.mult, ALU.add))
                v.wait_ge(ca.h, ca0 + NA * i + 3)
                vw(); vo(v.scalar_tensor_tensor(nmr[:], cm[:], -1.0 / 1024.0, rstd[:],
                                                ALU.mult, ALU.mult))
                vw(); vo(v.tensor_scalar(xn32[:], xti[:], rstd[:, 0:1], nmr[:, 0:1],
                                         ALU.mult, ALU.add))
                if i >= 2:
                    v.wait_ge(sgw[i % 2].h, 32 * ((i - 2) // 2 + 1))
                vw(); vo(v.tensor_copy(xnhi[i % 2][:], xn32[:]))
                vw(); vo(v.tensor_tensor(xnlo[i % 2][:], xn32[:], xnhi[i % 2][:],
                                         ALU.subtract))

        @block.scalar
        def _(s):
            ao, aw = mkao(cx, s)
            for i in range(NT):
                s.wait_ge(sdx[i % 3].h, 16 * (i // 3 + 1))
                aw(); ao(s.activation(sq[:], xt[i % 3][:], AF.Square))
                s.wait_ge(cv.h, cv0 + NV * i + 5)
                aw(); ao(s.activation(cvv[:], cvv[:], AF.Ln))
                aw(); ao(s.activation(rstd[:], cvv[:], AF.Exp, scale=-0.5))
        assert cv.val == cv0 + NV * NT
        assert ca.val == ca0 + NA * NT

        @block.gpsimd
        def _(g):
            for i in range(NT):
                g.wait_ge(cv.h, cv0 + NV * i + 8)
                g.dma_start(out=cx.D["dA"].ap()[i * 128:(i + 1) * 128, :],
                            in_=xnhi[i % 2][:]).then_inc(sgw[i % 2].h, 16)
                sgw[i % 2].val += 16
                g.wait_ge(cv.h, cv0 + NV * i + 9)
                g.dma_start(out=cx.D["dA2"].ap()[i * 128:(i + 1) * 128, :],
                            in_=xnlo[i % 2][:]).then_inc(sgw[i % 2].h, 16)
                sgw[i % 2].val += 16
    cx.p0_done = dict(sgw=[x.val for x in sgw], sgw_sems=sgw)


def emit_p0b(cx):
    block, D = cx.block, cx.D
    sa, sg, sd = cx.sa, cx.sg, cx.sd

    sa0 = sa.val
    @block.scalar
    def _(s):
        for sw in cx.p0_done["sgw_sems"]:
            s.wait_ge(sw.h, sw.val)
        for hc in range(8):
            src = ap(D["dA"], hc * 128, [(H, L), (1, 128)])
            s.dma_start(out=cx.xnF[:, hc * L:(hc + 1) * L], in_=src,
                        transpose=True).then_inc(sa.h, 16)
    sa.val += 16 * 8

    @block.gpsimd
    def _(g):
        g.wait_ge(sa.h, sa.val)
        for hc in range(8):
            g.dma_start(out=D["dB"].ap()[hc * 128:(hc + 1) * 128, :],
                        in_=cx.xnF[:, hc * L:(hc + 1) * L]).then_inc(sg.h, 16)
    sg.val += 16 * 8
    cx.p0b_done = dict(sa=sa.val, sg=sg.val)


def emit_p1(cx):
    nc, block, I, D = cx.nc, cx.block, cx.I, cx.D
    sd, sg, sa, cv, ca, ct = cx.sd, cx.sg, cx.sa, cx.cv, cx.ca, cx.ct
    S, PS = cx.S, cx.PS

    p1 = contextlib.ExitStack()
    with p1:
        S1 = lambda nm, shp, dt=F32: p1.enter_context(nc.sbuf_tensor(nm, shp, dt))
        gwf = S1("p1_gwf", [128, 128])
        gwhi = S1("p1_gwhi", [128, 128], BF16)
        gwlo = S1("p1_gwlo", [128, 128], BF16)
        xlo = [S1(f"p1_xlo{i}", [128, L], BF16) for i in range(2)]
        sc = S1("p1_sc", [128, 512])
        exn = S1("p1_ex", [128, 512])
        m32 = S1("p1_m32", [128, 32]); s32 = S1("p1_s32", [128, 32]); r32 = S1("p1_r32", [128, 32])
        orig = S1("p1_orig", [128, 512]); bsd = S1("p1_bsd", [128, 512])
        gsum = [S1(f"p1_gs{i}", [128, 128]) for i in range(2)]
        gsc = S1("p1_gsc", [128, 128])
        g32a = S1("p1_g32a", [128, 32]); g32b = S1("p1_g32b", [128, 32])
        g32c = S1("p1_g32c", [128, 32]); sec = S1("p1_sec", [128, 32])
        keep = S1("p1_keep", [128, 128])
        msk = S1("p1_msk", [128, 512]); msk2 = S1("p1_msk2", [128, 512])
        sel = S1("p1_sel", [128, 512])
        cmb = S1("p1_cmb", [128, 4096], BF16)

        sd0 = sd.val
        @block.sync
        def _(sync):
            quiesce(cx, sync)
            sync.dma_start(out=gwf[:],
                           in_=ap(I["gw_t"], 0, [(16, 128), (128 * 16, 8), (1, 16)])
                           ).then_inc(sd.h, 16)
        sd.val += 16

        @block.vector
        def _(v):
            vo, vw = mkvo(cx, v)
            quiesce(cx, v)
            v.wait_ge(sd.h, sd0 + 16)
            vw(); vo(v.tensor_copy(gwhi[:], gwf[:]))
            vw(); vo(v.tensor_tensor(gwlo[:], gwf[:], gwhi[:], ALU.subtract))
        cv_gw = cx.cv.val

        sxl = [cx.newsem(f"p1xl{j}") for j in range(2)]
        ct0 = ct.val
        @block.scalar
        def _(s):
            quiesce(cx, s)
            for dc in range(8):
                if dc >= 2:
                    s.wait_ge(ct.h, ct0 + dc - 1)
                src = ap(D["dA2"], dc * 128, [(H, L), (1, 128)])
                s.dma_start(out=xlo[dc % 2][:], in_=src, transpose=True).then_inc(sxl[dc % 2].h, 16)
                sxl[dc % 2].val += 16

        pband = PS[:, 0:512]
        @block.tensor
        def _(t):
            t.wait_ge(cx.cv.h, cv_gw)
            t.wait_ge(cx.sa.h, cx.p0b_done["sa"])
            first = True
            for dc in range(8):
                t.wait_ge(sxl[dc % 2].h, 16 * (dc // 2 + 1))
                for tc in range(32):
                    lhs_hi = cx.xnF[:, dc * L + tc * 128: dc * L + tc * 128 + 128]
                    lhs_lo = xlo[dc % 2][:, tc * 128:tc * 128 + 128]
                    rhs_hi = gwhi[:, dc * 16:dc * 16 + 16]
                    rhs_lo = gwlo[:, dc * 16:dc * 16 + 16]
                    og = pband[:, tc * 16:tc * 16 + 16]
                    last = (dc == 7) and (tc == 31)
                    t.matmul(og, lhs_hi, rhs_hi, start=first, stop=False)
                    first = False
                    t.matmul(og, lhs_hi, rhs_lo, start=False, stop=False)
                    ins = t.matmul(og, lhs_lo, rhs_hi, start=False, stop=last)
                    if tc == 31:
                        ins.then_inc(ct.h, 1)
        ct.val += 8

        @block.vector
        def _(v):
            vo, vw = mkvo(cx, v)
            vw(); vo(v.memset(cmb[:], 0.0))
            v.wait_ge(ct.h, ct.val)
            src3 = ap(cx.PS, 0, [(4096, 128), (16, 32), (1, 16)])
            vw(); vo(v.tensor_reduce(m32[:], src3, AX.X, ALU.max))
            mb = ap(m32, 0, [(32, 128), (1, 32), (0, 16)])
            vw(); vo(v.tensor_tensor(sc[:], pband, mb, ALU.subtract))
        @block.scalar
        def _(s):
            ao, aw = mkao(cx, s)
            s.wait_ge(cx.cv.h, cx.cv.val)
            aw(); ao(s.activation(exn[:], sc[:], AF.Exp))
        @block.vector
        def _(v):
            vo, vw = mkvo(cx, v)
            v.wait_ge(cx.ca.h, cx.ca.val)
            e3 = ap(exn, 0, [(512, 128), (16, 32), (1, 16)])
            vw(); vo(v.tensor_reduce(s32[:], e3, AX.X, ALU.add))
            vw(); vo(v.reciprocal(r32[:], s32[:]))
            rb = ap(r32, 0, [(32, 128), (1, 32), (0, 16)])
            vw(); vo(v.tensor_tensor(orig[:], exn[:], rb, ALU.mult))
            gbb = ap(cx.sm["gb"], 0, [(16, 128), (0, 32), (1, 16)])
            vw(); vo(v.tensor_tensor(bsd[:], orig[:], gbb, ALU.add))
            def gview(t, j):
                return ap(t, j, [(512, 128), (16, 32), (4, 4)])
            pair = 0
            firstp = True
            for i in range(4):
                for j in range(i + 1, 4):
                    dst = gsum[pair % 2]
                    vw(); vo(v.tensor_tensor(dst[:], gview(bsd, i), gview(bsd, j), ALU.add))
                    if firstp:
                        vw(); vo(v.tensor_copy(gsc[:], dst[:]))
                        firstp = False
                    else:
                        vw(); vo(v.tensor_tensor(gsc[:], gsc[:], dst[:], ALU.max))
                    pair += 1
            def cview(t, g):
                return ap(t, g, [(128, 128), (4, 32)])
            vw(); vo(v.tensor_tensor(g32a[:, 0:32], cview(gsc, 0), cview(gsc, 1), ALU.max))
            vw(); vo(v.tensor_tensor(g32b[:, 0:32], cview(gsc, 0), cview(gsc, 1), ALU.min))
            vw(); vo(v.tensor_tensor(g32c[:, 0:32], cview(gsc, 2), cview(gsc, 3), ALU.max))
            vw(); vo(v.tensor_tensor(sec[:, 0:32], cview(gsc, 2), cview(gsc, 3), ALU.min))
            vw(); vo(v.tensor_tensor(g32a[:, 0:32], g32a[:, 0:32], g32c[:, 0:32], ALU.min))
            vw(); vo(v.tensor_tensor(g32b[:, 0:32], g32b[:, 0:32], sec[:, 0:32], ALU.max))
            vw(); vo(v.tensor_tensor(sec[:, 0:32], g32a[:, 0:32], g32b[:, 0:32], ALU.max))
            secb = ap(sec, 0, [(32, 128), (1, 32), (0, 4)])
            vw(); vo(v.tensor_tensor(keep[:], gsc[:], secb, ALU.is_ge))
            kb = ap(keep, 0, [(128, 128), (4, 32), (1, 4), (0, 4)])
            vw(); vo(v.tensor_scalar(msk[:], kb, 1e9, -1e9, ALU.mult, ALU.add))
            vw(); vo(v.tensor_tensor(msk[:], msk[:], bsd[:], ALU.add))
            m3 = ap(msk, 0, [(512, 128), (16, 32), (1, 16)])
            vw(); vo(v.tensor_reduce(m32[:], m3, AX.X, ALU.max))
            vw(); vo(v.tensor_copy(g32c[:, 0:32], m32[:]))
            mb1 = ap(m32, 0, [(32, 128), (1, 32), (0, 16)])
            vw(); vo(v.tensor_tensor(sel[:], msk[:], mb1, ALU.is_ge))
            vw(); vo(v.tensor_copy(gsc[:], sel[:, 0:128]))
            vw(); vo(v.scalar_tensor_tensor(msk2[:], sel[:], -1e9, msk[:], ALU.mult, ALU.add))
            m23 = ap(msk2, 0, [(512, 128), (16, 32), (1, 16)])
            vw(); vo(v.tensor_reduce(s32[:], m23, AX.X, ALU.max))
            sb2 = ap(s32, 0, [(32, 128), (1, 32), (0, 16)])
            vw(); vo(v.tensor_tensor(sel[:], msk[:], sb2, ALU.is_ge))
            cdst = ap(cmb, 0, [(4096, 128), (128, 32), (1, 16)])
            vw(); vo(v.tensor_tensor(cdst, sel[:], orig[:], ALU.mult))
        cx.p1_cmb_cv = cx.cv.val

        if "dbg0" in cx.taps:
            @block.gpsimd
            def _(g):
                g.wait_ge(cx.cv.h, cx.p1_cmb_cv)
                for j, t in enumerate([sc, exn, orig, bsd, msk, sel, None, None, msk2, m32, s32]):
                    if t is None:
                        continue
                    g.dma_start(out=cx.taps[f"dbg{j}"].ap(), in_=t[:]).then_inc(sg.h, 16)
                    sg.val += 16
                g.dma_start(out=cx.taps["dbg6"].ap(), in_=gsc[:]).then_inc(sg.h, 16)
                g.dma_start(out=cx.taps["dbg7"].ap(), in_=g32c[:, 0:32]).then_inc(sg.h, 16)
                sg.val += 32

        sg0 = sg.val
        @block.gpsimd
        def _(g):
            g.wait_ge(cx.cv.h, cx.p1_cmb_cv)
            dst = ap(D["dCMB"], 0, [(128, 128), (128 * 128, 32), (1, 128)])
            g.dma_start(out=dst, in_=cmb[:]).then_inc(sg.h, 16)
        sg.val += 16

        @block.scalar
        def _(s):
            s.wait_ge(sg.h, sg.val)
            s.wait_ge(cx.cv.h, 1)
            s.dma_start(out=cx.cmbT[0:128, :], in_=D["dCMB"].ap(), transpose=True
                        ).then_inc(sa.h, 16)
        sa.val += 16

        @block.gpsimd
        def _(g):
            g.wait_ge(sa.h, sa.val)
            g.dma_start(out=D["dCMBT"].ap(), in_=cx.cmbT[0:16, :]).then_inc(sg.h, 16)
        sg.val += 16
    cx.p1_done = dict(sg=sg.val, sa=sa.val, cv=cx.cv.val)


def fftf_pe(cx, t, m2buf, m2off, GB, XB, dve_ready, grbuf, gibuf, xb_free_ca=None):
    ct, cst = cx.ct, cx.cst
    for pr in range(4):
        bank = GB[pr // 2]
        og = bank[:, (pr % 2) * 256:(pr % 2) * 256 + 256]
        ins = t.matmul(og, m2buf[:, m2off + pr * 128: m2off + pr * 128 + 128],
                       cst["cf"][:], start=(pr % 2 == 0), stop=(pr % 2 == 1))
        if pr == 3:
            ins.then_inc(ct.h, 1)
    ct.val += 1
    t.wait_ge(cx.cv.h, dve_ready)
    if xb_free_ca is not None:
        t.wait_ge(cx.ca.h, xb_free_ca)
    seq = [("s2wr", grbuf, 0, "start"), ("s2win", gibuf, 0, "stop"),
           ("s2wi", grbuf, 1, "start"), ("s2wr", gibuf, 1, "stop")]
    ins = None
    for cname, src, xbank, role in seq:
        for sl in range(2):
            ins = t.matmul(XB[xbank][:, sl * 256:sl * 256 + 256],
                           cst[cname][:], src[:, sl * 256:sl * 256 + 256],
                           start=(role == "start" and sl == 0),
                           stop=(role == "stop" and sl == 1))
    ins.then_inc(ct.h, 1)
    ct.val += 1


def fftf_act(cx, s, GB, XB, gbuf, xbuf, pe_base, ao, aw):
    s.wait_ge(cx.ct.h, pe_base + 1)
    aw(); ao(s.activation(gbuf[:, 0:512], GB[0][:], AF.Copy))
    aw(); ao(s.activation(gbuf[:, 512:1024], GB[1][:], AF.Copy))
    s.wait_ge(cx.ct.h, pe_base + 2)
    aw(); ao(s.activation(xbuf[:, 0:512], XB[0][:], AF.Copy))
    aw(); ao(s.activation(xbuf[:, 512:1024], XB[1][:], AF.Copy))


def fftf_dve(cx, v, gbuf, grbuf, gibuf, tw1, tw2, act_base, vo, vw):
    grv = ap(gbuf, 0, [(1024, 128), (256, 4), (1, 128)])
    giv = ap(gbuf, 128, [(1024, 128), (256, 4), (1, 128)])
    t1rb = ap(cx.cst["t1r"], 0, [(128, 128), (0, 4), (1, 128)])
    t1ib = ap(cx.cst["t1i"], 0, [(128, 128), (0, 4), (1, 128)])
    v.wait_ge(cx.ca.h, act_base + 2)
    vw(); vo(v.tensor_tensor(tw1[:], grv, t1rb, ALU.mult))
    vw(); vo(v.tensor_tensor(tw2[:], giv, t1ib, ALU.mult))
    vw(); vo(v.tensor_tensor(grbuf[:], tw1[:], tw2[:], ALU.subtract))
    vw(); vo(v.tensor_tensor(tw1[:], grv, t1ib, ALU.mult))
    vw(); vo(v.tensor_tensor(tw2[:], giv, t1rb, ALU.mult))
    vw(); vo(v.tensor_tensor(gibuf[:], tw1[:], tw2[:], ALU.add))


def clamppi(v, vo, vw, buf):
    vw(); vo(v.tensor_scalar(buf[:], buf[:], math.pi, None, ALU.min))
    vw(); vo(v.tensor_scalar(buf[:], buf[:], -math.pi, None, ALU.max))


def emit_p2(cx):
    nc, block, I, D = cx.nc, cx.block, cx.I, cx.D
    sd, sg, cv, ca, ct = cx.sd, cx.sg, cx.cv, cx.ca, cx.ct
    PS = cx.PS
    HH = H

    st5 = p5_setup(cx)
    p2 = contextlib.ExitStack()
    with p2:
        S2_ = lambda nm, shp, dt=F32: p2.enter_context(nc.sbuf_tensor(nm, shp, dt))
        ucr = S2_("p2_ucr", [64, H]); uci = S2_("p2_uci", [64, H])
        BM = {nm: S2_("p2c_" + nm, [64, 256], BF16) for nm in
              ["pr", "pi", "wr", "wi", "qr", "qi", "ar", "ai"]}
        Ur = S2_("p2_Ur", [128, 2048], BF16)
        Ui = S2_("p2_Ui", [128, 2048], BF16)
        Vr = S2_("p2_Vr", [128, 2048], BF16)
        Vi = S2_("p2_Vi", [128, 2048], BF16)
        km2 = S2_("p2_km2", [128, 2048], BF16)
        gbuf = [S2_(f"p2_gbuf{j}", [128, 1024], BF16) for j in range(2)]
        grbuf = [S2_(f"p2_grb{j}", [128, 512], BF16) for j in range(2)]
        gibuf = [S2_(f"p2_gib{j}", [128, 512], BF16) for j in range(2)]
        xbuf = S2_("p2_xbuf", [128, 1024], BF16)
        tw1 = S2_("p2_tw1", [128, 512], BF16)
        tw2 = S2_("p2_tw2", [128, 512], BF16)

        dWP = D["dWP"]
        swk = cx.newsem("p2wk")
        sgk = cx.newsem("p2k")
        swp = cx.newsem("p2wp")
        sdt = cx.newsem("p2dt")
        cx.p2_sgk = sgk

        with contextlib.ExitStack() as pp:
            SP = lambda nm, shp, dt=F32: pp.enter_context(nc.sbuf_tensor(nm, shp, dt))
            T = {nm: SP("p2_" + nm, [64, HH]) for nm in
                 ["dtb", "rho", "th", "r0", "r1", "r2", "r3", "r4", "r5", "r6"]}
            ldt = SP("p2_ldt", [1, H])

            sd0 = sd.val
            @block.scalar
            def _(s):
                ao, aw = mkao(cx, s)
                quiesce(cx, s)
            @block.gpsimd
            def _(g):
                quiesce(cx, g)
            @block.vector
            def _(v):
                vo, vw = mkvo(cx, v)
                quiesce(cx, v)
            sld = cx.newsem("p2ld")
            skp = cx.newsem("p2in")
            @block.sync
            def _(sync):
                quiesce(cx, sync)
                sync.dma_start(out=ldt[:], in_=I["logdt_t"].ap()).then_inc(sld.h, 16)
                sld.val += 16
            @block.scalar
            def _(s):
                ao, aw = mkao(cx, s)
                s.wait_ge(sld.h, 16)
                aw(); ao(s.activation(ldt[:], ldt[:], AF.Exp))
            @block.gpsimd
            def _(g):
                g.wait_ge(ca.h, ca.val)
                g.dma_start(out=D["dDT"].ap(), in_=ldt[0:1, :]).then_inc(sdt.h, 16)
                sdt.val += 16

            def redcos(v, vo, vw, src, ang, msk):
                vw(); vo(v.tensor_scalar(ang[:], src[:], 1.0 / TWO_PI, MAGIC, ALU.mult, ALU.add))
                vw(); vo(v.tensor_scalar(ang[:], ang[:], MAGIC, None, ALU.subtract))
                vw(); vo(v.scalar_tensor_tensor(msk[:], ang[:], -RC1, src[:], ALU.mult, ALU.add))
                vw(); vo(v.scalar_tensor_tensor(ang[:], ang[:], -RC2, msk[:], ALU.mult, ALU.add))
                clamppi(v, vo, vw, ang)
                vw(); vo(v.tensor_scalar(msk[:], ang[:], math.pi / 2.0, None, ALU.is_gt))
                vw(); vo(v.scalar_tensor_tensor(msk[:], msk[:], -TWO_PI, ang[:], ALU.mult, ALU.add))
                vw(); vo(v.tensor_scalar(msk[:], msk[:], math.pi / 2.0, None, ALU.add))
                clamppi(v, vo, vw, msk)

            for hf in range(1):
                h0 = hf * HH
                @block.sync
                def _(sync):
                    if hf > 0:
                        sync.wait_ge(swp.h, swp.val)
                    for nm, inm in [("r1", "arl_t"), ("r2", "aim_t"), ("r3", "cr_t"),
                                    ("r4", "ci_t")]:
                        sync.dma_start(out=T[nm][:],
                                       in_=I[inm].ap()[:, h0:h0 + HH]).then_inc(skp.h, 16)
                        skp.val += 16
                @block.gpsimd
                def _(g):
                    g.wait_ge(sdt.h, 16)
                    if hf > 0:
                        g.wait_ge(swp.h, swp.val)
                    g.dma_start(out=T["dtb"][:],
                                in_=ap(D["dDT"], h0, [(0, 64), (1, HH)])).then_inc(sdt.h, 16)
                    sdt.val += 16
                @block.scalar
                def _(s):
                    ao, aw = mkao(cx, s)
                    s.wait_ge(skp.h, skp.val)
                    if hf > 0:
                        s.wait_ge(cv.h, cv.val)
                    aw(); ao(s.activation(T["r0"][:], T["r1"][:], AF.Exp))
                @block.vector
                def _(v):
                    vo, vw = mkvo(cx, v)
                    m = T
                    v.wait_ge(sdt.h, sdt.val)
                    v.wait_ge(ca.h, ca.val)
                    vw(); vo(v.scalar_tensor_tensor(m["rho"][:], m["r0"][:], -1.0, m["dtb"][:], ALU.mult, ALU.mult))
                    vw(); vo(v.tensor_tensor(m["th"][:], m["r2"][:], m["dtb"][:], ALU.mult))
                    redcos(v, vo, vw, m["th"], m["r5"], m["r6"])
                @block.scalar
                def _(s):
                    ao, aw = mkao(cx, s)
                    s.wait_ge(cv.h, cv.val)
                    aw(); ao(s.activation(T["r5"][:], T["r5"][:], AF.Sin))
                    aw(); ao(s.activation(T["r6"][:], T["r6"][:], AF.Sin))
                    aw(); ao(s.activation(T["r1"][:], T["rho"][:], AF.Exp))
                @block.vector
                def _(v):
                    vo, vw = mkvo(cx, v)
                    m = T
                    ucr_h = ucr[:, h0:h0 + HH]
                    uci_h = uci[:, h0:h0 + HH]
                    v.wait_ge(ca.h, ca.val)
                    vw(); vo(v.tensor_tensor(m["r6"][:], m["r1"][:], m["r6"][:], ALU.mult))
                    vw(); vo(v.tensor_scalar(m["r6"][:], m["r6"][:], 1.0, None, ALU.subtract))
                    vw(); vo(v.tensor_tensor(m["r5"][:], m["r1"][:], m["r5"][:], ALU.mult))
                    vw(); vo(v.reciprocal(m["r1"][:], m["dtb"][:]))
                    vw(); vo(v.tensor_tensor(m["r1"][:], m["th"][:], m["r1"][:], ALU.mult))
                    vw(); vo(v.tensor_tensor(m["r2"][:], m["r0"][:], m["r0"][:], ALU.mult))
                    vw(); vo(v.tensor_tensor(ucr_h, m["r1"][:], m["r1"][:], ALU.mult))
                    vw(); vo(v.tensor_tensor(m["r2"][:], m["r2"][:], ucr_h, ALU.add))
                    vw(); vo(v.reciprocal(m["r2"][:], m["r2"][:]))
                    vw(); vo(v.tensor_tensor(ucr_h, m["r3"][:], m["r6"][:], ALU.mult))
                    vw(); vo(v.tensor_tensor(uci_h, m["r4"][:], m["r5"][:], ALU.mult))
                    vw(); vo(v.tensor_tensor(ucr_h, ucr_h, uci_h, ALU.subtract))
                    vw(); vo(v.tensor_tensor(uci_h, m["r3"][:], m["r5"][:], ALU.mult))
                    vw(); vo(v.tensor_tensor(m["r3"][:], m["r4"][:], m["r6"][:], ALU.mult))
                    vw(); vo(v.tensor_tensor(uci_h, uci_h, m["r3"][:], ALU.add))
                    vw(); vo(v.tensor_tensor(m["r3"][:], uci_h, m["r1"][:], ALU.mult))
                    vw(); vo(v.tensor_tensor(m["r4"][:], ucr_h, m["r0"][:], ALU.mult))
                    vw(); vo(v.tensor_tensor(m["r3"][:], m["r3"][:], m["r4"][:], ALU.subtract))
                    vw(); vo(v.tensor_tensor(m["r4"][:], ucr_h, m["r1"][:], ALU.mult))
                    vw(); vo(v.tensor_tensor(m["r5"][:], uci_h, m["r0"][:], ALU.mult))
                    vw(); vo(v.tensor_tensor(m["r4"][:], m["r4"][:], m["r5"][:], ALU.add))
                    vw(); vo(v.tensor_tensor(ucr_h, m["r3"][:], m["r2"][:], ALU.mult))
                    vw(); vo(v.tensor_scalar(ucr_h, ucr_h, 2.0, None, ALU.mult))
                    vw(); vo(v.tensor_tensor(uci_h, m["r4"][:], m["r2"][:], ALU.mult))
                    vw(); vo(v.tensor_scalar(uci_h, uci_h, 2.0, None, ALU.mult))
                for k in range(12):
                    @block.vector
                    def _(v):
                        vo, vw = mkvo(cx, v)
                        m = T
                        v.wait_ge(ca.h, ca.val)
                        if k > 0 or hf > 0:
                            v.wait_ge(swp.h, swp.val)
                        sc = float(2 ** k)
                        vw(); vo(v.tensor_scalar(m["r1"][:], m["th"][:], sc, None, ALU.mult))
                        redcos(v, vo, vw, m["r1"], m["r3"], m["r4"])
                        vw(); vo(v.tensor_scalar(m["r5"][:], m["rho"][:], sc, None, ALU.mult))
                    @block.scalar
                    def _(s):
                        ao, aw = mkao(cx, s)
                        s.wait_ge(cv.h, cv.val)
                        aw(); ao(s.activation(T["r3"][:], T["r3"][:], AF.Sin))
                        aw(); ao(s.activation(T["r4"][:], T["r4"][:], AF.Sin))
                        aw(); ao(s.activation(T["r5"][:], T["r5"][:], AF.Exp))
                    @block.vector
                    def _(v):
                        vo, vw = mkvo(cx, v)
                        m = T
                        v.wait_ge(ca.h, ca.val)
                        vw(); vo(v.tensor_tensor(m["r1"][:], m["r5"][:], m["r4"][:], ALU.mult))
                        vw(); vo(v.scalar_tensor_tensor(m["r2"][:], m["r5"][:], -1.0, m["r3"][:], ALU.mult, ALU.mult))
                    @block.gpsimd
                    def _(g):
                        g.wait_ge(cv.h, cv.val)
                        g.dma_start(out=dWP.ap()[(2 * k) * 64:(2 * k) * 64 + 64, h0:h0 + HH],
                                    in_=T["r1"][:]).then_inc(swp.h, 16)
                        g.dma_start(out=dWP.ap()[(2 * k + 1) * 64:(2 * k + 1) * 64 + 64, h0:h0 + HH],
                                    in_=T["r2"][:]).then_inc(swp.h, 16)
                        swp.val += 32

        tA = S2_("p2_tA", [64, 2048], BF16); tB = S2_("p2_tB", [64, 2048], BF16)
        tC = S2_("p2_tC", [64, 2048], BF16); tD = S2_("p2_tD", [64, 2048], BF16)

        dF8 = D["dF8"]
        sfb = cx.newsem("p2fb")
        sld2 = cx.newsem("p2ml")
        with contextlib.ExitStack() as pb:
            SB = lambda nm, shp, dt=F32: pb.enter_context(nc.sbuf_tensor(nm, shp, dt))
            tbr = SB("p2_tbr", [64, 2048]); tbi = SB("p2_tbi", [64, 2048])
            mr3 = SB("p2_mr3", [64, 768]); mi3 = SB("p2_mi3", [64, 768])
            tp1b = SB("p2_t1b", [64, 1024]); tp2b = SB("p2_t2b", [64, 1024])
            tabs = [(0, True, 0), (3, True, 2), (6, False, 4), (9, False, 6)]
            nfb = 0
            for c in range(4):
                for (kbase, cj, pl) in tabs:
                    @block.sync
                    def _(sync, c=c, kbase=kbase, nfb=nfb):
                        if nfb == 0:
                            sync.wait_ge(swp.h, swp.val)
                        else:
                            sync.wait_ge(cv.h, cv.val)
                        for st_ in range(3):
                            k = kbase + st_
                            sync.dma_start(out=mr3[:, st_ * 256:st_ * 256 + 256],
                                           in_=dWP.ap()[(2 * k) * 64:(2 * k) * 64 + 64,
                                                        c * 256:c * 256 + 256]).then_inc(sld2.h, 16)
                            sync.dma_start(out=mi3[:, st_ * 256:st_ * 256 + 256],
                                           in_=dWP.ap()[(2 * k + 1) * 64:(2 * k + 1) * 64 + 64,
                                                        c * 256:c * 256 + 256]).then_inc(sld2.h, 16)
                            sld2.val += 32
                    @block.vector
                    def _(v, cj=cj, nfb=nfb):
                        vo, vw = mkvo(cx, v)
                        v.wait_ge(sld2.h, sld2.val)
                        if nfb >= 1:
                            v.wait_ge(sfb.h, 32 * nfb)
                        vw(); vo(v.memset(ap(tbr, 0, [(2048, 64), (8, 256), (1, 1)]), 1.0))
                        vw(); vo(v.memset(ap(tbi, 0, [(2048, 64), (8, 256), (1, 1)]), 0.0))
                        for st_ in range(3):
                            Tn = 2 ** st_
                            sr = ap(tbr, 0, [(2048, 64), (8, 256), (1, Tn)])
                            si = ap(tbi, 0, [(2048, 64), (8, 256), (1, Tn)])
                            dr = ap(tbr, Tn, [(2048, 64), (8, 256), (1, Tn)])
                            di = ap(tbi, Tn, [(2048, 64), (8, 256), (1, Tn)])
                            wr_ = ap(mr3, st_ * 256, [(768, 64), (1, 256), (0, Tn)])
                            wi_ = ap(mi3, st_ * 256, [(768, 64), (1, 256), (0, Tn)])
                            t1v = ap(tp1b, 0, [(1024, 64), (Tn, 256), (1, Tn)])
                            t2v = ap(tp2b, 0, [(1024, 64), (Tn, 256), (1, Tn)])
                            vw(); vo(v.tensor_tensor(t1v, sr, wr_, ALU.mult))
                            vw(); vo(v.tensor_tensor(t2v, si, wi_, ALU.mult))
                            vw(); vo(v.tensor_tensor(dr, t1v, t2v,
                                                     ALU.subtract if cj else ALU.add))
                            vw(); vo(v.tensor_tensor(t1v, sr, wi_, ALU.mult))
                            vw(); vo(v.tensor_tensor(t2v, si, wr_, ALU.mult))
                            if cj:
                                vw(); vo(v.tensor_tensor(di, t1v, t2v, ALU.add))
                            else:
                                vw(); vo(v.tensor_tensor(di, t2v, t1v, ALU.subtract))
                    @block.gpsimd
                    def _(g, c=c, pl=pl):
                        g.wait_ge(cv.h, cv.val)
                        g.dma_start(out=ap(dF8, pl * 8192 + c * 2048, [(65536, 64), (1, 2048)]),
                                    in_=tbr[:]).then_inc(sfb.h, 16)
                        g.dma_start(out=ap(dF8, (pl + 1) * 8192 + c * 2048, [(65536, 64), (1, 2048)]),
                                    in_=tbi[:]).then_inc(sfb.h, 16)
                        sfb.val += 32
                    nfb += 1

        for c5 in range(8):
            p5_chunk(cx, st5, c5)

        NKG = 32
        DPEg = 4 + 8
        DAg = 4 + 16
        DVg = 18 + 24
        ct0 = ct.val
        ca0 = ca.val

        @block.vector
        def _(v):
            vo, vw = mkvo(cx, v)
            for bb in [Ur, Ui, Vr, Vi, km2]:
                vw(); vo(v.memset(bb[:], 0.0))
        cv0 = cv.val

        FPL = ["pr", "pi", "wr", "wi", "qr", "qi", "ar", "ai"]

        @block.sync
        def _(sync):
            sync.wait_ge(sfb.h, sfb.val)
            for g_ in range(NKG):
                if g_ >= 1:
                    sync.wait_ge(cv.h, cv0 + (g_ - 1) * DVg + 18)
                    sync.wait_ge(swk.h, swk.val)
                for pl, nm in enumerate(FPL):
                    src = ap(dF8, pl * 8192 + g_ * 256, [(65536, 64), (1, 256)])
                    sync.dma_start(out=BM[nm][:], in_=src).then_inc(swk.h, 16)
                    swk.val += 16

        GB = (PS[:, 0:512], PS[:, 512:1024])
        XB = (PS[:, 1024:1536], PS[:, 1536:2048])

        for g_ in range(NKG):
            v0 = cv0 + g_ * DVg
            a0g = ca0 + g_ * DAg
            pe0g = ct0 + g_ * DPEg
            @block.vector
            def _(v):
                vo, vw = mkvo(cx, v)
                v.wait_ge(swk.h, 128 * (g_ + 1))
                if g_ > 0:
                    v.wait_ge(ct.h, pe0g - DPEg + 4)
                bview = lambda t_: ap(BM[t_], 0, [(256, 64), (8, 32), (0, 8), (1, 8)])
                aview = lambda t_: ap(BM[t_], 0, [(256, 64), (8, 32), (1, 8), (0, 8)])
                sview = lambda t_: ap(t_, g_ * 32, [(H, 64), (1, 32), (0, 8), (0, 8)])
                tAv = ap(tA, 0, [(2048, 64), (1, 2048)])
                tBv = ap(tB, 0, [(2048, 64), (1, 2048)])
                tCv = ap(tC, 0, [(2048, 64), (1, 2048)])
                tDv = ap(tD, 0, [(2048, 64), (1, 2048)])
                vw(); vo(v.tensor_tensor(tAv, bview("pr"), aview("wr"), ALU.mult))
                vw(); vo(v.tensor_tensor(tBv, bview("pi"), aview("wi"), ALU.mult))
                vw(); vo(v.tensor_tensor(tAv, tAv, tBv, ALU.subtract))
                vw(); vo(v.tensor_tensor(tBv, bview("pr"), aview("wi"), ALU.mult))
                vw(); vo(v.tensor_tensor(tCv, bview("pi"), aview("wr"), ALU.mult))
                vw(); vo(v.tensor_tensor(tBv, tBv, tCv, ALU.add))
                vw(); vo(v.tensor_tensor(tCv, sview(ucr), tAv, ALU.mult))
                vw(); vo(v.tensor_tensor(tDv, sview(uci), tBv, ALU.mult))
                vw(); vo(v.tensor_tensor(Ur[0:64, :], tCv, tDv, ALU.subtract))
                vw(); vo(v.tensor_tensor(tCv, sview(ucr), tBv, ALU.mult))
                vw(); vo(v.tensor_tensor(tDv, sview(uci), tAv, ALU.mult))
                vw(); vo(v.tensor_tensor(Ui[0:64, :], tCv, tDv, ALU.add))
                vw(); vo(v.tensor_tensor(tAv, bview("qr"), aview("ar"), ALU.mult))
                vw(); vo(v.tensor_tensor(tBv, bview("qi"), aview("ai"), ALU.mult))
                vw(); vo(v.tensor_tensor(Vr[0:64, :], tAv, tBv, ALU.subtract))
                vw(); vo(v.tensor_tensor(tAv, bview("qr"), aview("ai"), ALU.mult))
                vw(); vo(v.tensor_tensor(tBv, bview("qi"), aview("ar"), ALU.mult))
                vw(); vo(v.tensor_tensor(Vi[0:64, :], tAv, tBv, ALU.add))
            @block.tensor
            def _(t):
                if g_ == 0:
                    t.wait_ge(cv.h, v0 + 18)
                else:
                    t.wait_ge(cv.h, v0 + 18)
                    t.wait_ge(ca.h, a0g)
                for hl in range(32):
                    og = bass.AP(cx.PS, ((hl // 8) % 2) * 512 + (hl % 8) * 64,
                                 [[4096, 64], [1, 64]])
                    if hl % 8 == 0 and hl >= 16:
                        t.wait_ge(ca.h, a0g + hl // 8 - 1)
                    t.matmul(og, Vr[:, hl * 64:hl * 64 + 64],
                             Ur[:, hl * 64:hl * 64 + 64],
                             start=(hl % 8 == 0), stop=False)
                    ins = t.matmul(og, Vi[:, hl * 64:hl * 64 + 64],
                                   Ui[:, hl * 64:hl * 64 + 64],
                                   start=False, stop=(hl % 8 == 7))
                    if hl % 8 == 7:
                        ins.then_inc(ct.h, 1)
                        ct.val += 1
            @block.scalar
            def _(s):
                ao, aw = mkao(cx, s)
                if g_ > 0:
                    s.wait_ge(ct.h, pe0g)
                for gq in range(4):
                    s.wait_ge(ct.h, pe0g + gq + 1)
                    pb = bass.AP(cx.PS, (gq % 2) * 512, [[4096, 64], [1, 512]])
                    dst = bass.AP(km2, gq * 512, [[2048, 64], [1, 512]])
                    aw(); ao(s.activation(dst, pb, AF.Copy))
            K_ = pe0g + 4
            s1_ct = {0: K_ + 1, 1: K_ + 2, 2: K_ + 4, 3: K_ + 6}
            s2_ct = {0: K_ + 3, 1: K_ + 5, 2: K_ + 7, 3: K_ + 8}
            gev_end = {0: a0g + 6, 1: a0g + 8, 2: a0g + 12, 3: a0g + 16}
            xev_end = {0: a0g + 10, 1: a0g + 14, 2: a0g + 18, 3: a0g + 20}
            tw_cv = {b: v0 + 18 + 6 * (b + 1) for b in range(4)}
            sgk0 = sgk.val
            for it in range(5):
                @block.tensor
                def _(t, it=it):
                    if it <= 3:
                        t.wait_ge(ca.h, a0g + 4)
                        if it >= 1:
                            t.wait_ge(ca.h, gev_end[it - 1])
                        ins = None
                        for pr in range(4):
                            og = GB[pr // 2][:, (pr % 2) * 256:(pr % 2) * 256 + 256]
                            ins = t.matmul(og, km2[:, it * 512 + pr * 128: it * 512 + pr * 128 + 128],
                                           cx.cst["cf"][:], start=(pr % 2 == 0), stop=(pr % 2 == 1))
                        ins.then_inc(ct.h, 1); ct.val += 1
                    if it >= 1:
                        b = it - 1
                        t.wait_ge(cv.h, tw_cv[b])
                        if b >= 1:
                            t.wait_ge(ca.h, xev_end[b - 1])
                        elif g_ > 0:
                            t.wait_ge(ca.h, a0g)
                        seq = [("s2wr", grbuf[b % 2], 0, "start"), ("s2win", gibuf[b % 2], 0, "stop"),
                               ("s2wi", grbuf[b % 2], 1, "start"), ("s2wr", gibuf[b % 2], 1, "stop")]
                        ins = None
                        for cname, srcb, xbank, role in seq:
                            for sl in range(2):
                                ins = t.matmul(XB[xbank][:, sl * 256:sl * 256 + 256],
                                               cx.cst[cname][:], srcb[:, sl * 256:sl * 256 + 256],
                                               start=(role == "start" and sl == 0),
                                               stop=(role == "stop" and sl == 1))
                        ins.then_inc(ct.h, 1); ct.val += 1
                @block.scalar
                def _(s, it=it):
                    ao, aw = mkao(cx, s)
                    if it <= 3:
                        s.wait_ge(ct.h, s1_ct[it])
                        if it >= 2:
                            s.wait_ge(cv.h, tw_cv[it - 2])
                        elif g_ > 0:
                            s.wait_ge(cv.h, v0)
                        aw(); ao(s.activation(gbuf[it % 2][:, 0:512], GB[0][:], AF.Copy))
                        aw(); ao(s.activation(gbuf[it % 2][:, 512:1024], GB[1][:], AF.Copy))
                    if it >= 1:
                        b = it - 1
                        s.wait_ge(ct.h, s2_ct[b])
                        if b >= 1:
                            s.wait_ge(sgk.h, sgk0 + 128 * b)
                        elif g_ > 0:
                            s.wait_ge(sgk.h, sgk0)
                        aw(); ao(s.activation(xbuf[:, 0:512], XB[0][:], AF.Copy))
                        aw(); ao(s.activation(xbuf[:, 512:1024], XB[1][:], AF.Copy))
                @block.vector
                def _(v, it=it):
                    vo, vw = mkvo(cx, v)
                    if it <= 3:
                        v.wait_ge(ca.h, gev_end[it])
                        if it >= 2:
                            v.wait_ge(ct.h, s2_ct[it - 2])
                        elif g_ > 0:
                            v.wait_ge(ct.h, pe0g)
                        gb = gbuf[it % 2]
                        grv = ap(gb, 0, [(1024, 128), (256, 4), (1, 128)])
                        giv = ap(gb, 128, [(1024, 128), (256, 4), (1, 128)])
                        t1rb = ap(cx.cst["t1r"], 0, [(128, 128), (0, 4), (1, 128)])
                        t1ib = ap(cx.cst["t1i"], 0, [(128, 128), (0, 4), (1, 128)])
                        vw(); vo(v.tensor_tensor(tw1[:], grv, t1rb, ALU.mult))
                        vw(); vo(v.tensor_tensor(tw2[:], giv, t1ib, ALU.mult))
                        vw(); vo(v.tensor_tensor(grbuf[it % 2][:], tw1[:], tw2[:], ALU.subtract))
                        vw(); vo(v.tensor_tensor(tw1[:], grv, t1ib, ALU.mult))
                        vw(); vo(v.tensor_tensor(tw2[:], giv, t1rb, ALU.mult))
                        vw(); vo(v.tensor_tensor(gibuf[it % 2][:], tw1[:], tw2[:], ALU.add))
                @block.gpsimd
                def _(g, it=it):
                    if it >= 1:
                        b = it - 1
                        g.wait_ge(ca.h, xev_end[b])
                        for par in range(2):
                            for pr in range(4):
                                ch = g_ * 32 + b * 8 + 2 * pr + par
                                src = ap(xbuf, par * 64 * 1024 + pr * 128,
                                         [(1024, 64), (512, 2), (1, 128)])
                                dst = ap(D["dKLOC"], ch * 64 * 256,
                                         [(256, 64), (128, 2), (1, 128)])
                                g.dma_start(out=dst, in_=src).then_inc(sgk.h, 16)
                                sgk.val += 16
            for c5 in range(8 + (g_ * 20) // 32, 8 + ((g_ + 1) * 20) // 32):
                p5_chunk(cx, st5, c5)
    cx.st5 = st5
    cx.p2_done = True


def emit_p3(cx):
    nc, block, I, D = cx.nc, cx.block, cx.I, cx.D
    sd, sg, sa, cv, ca, ct = cx.sd, cx.sg, cx.sa, cx.cv, cx.ca, cx.ct
    PS = cx.PS

    p3 = contextlib.ExitStack()
    with p3:
        S3 = lambda nm, shp, dt=F32: p3.enter_context(nc.sbuf_tensor(nm, shp, dt))
        NB = 128
        m2b = [S3(f"p3_m2{j}", [128, 512], BF16) for j in range(2)]
        kb = [S3(f"p3_k{j}", [128, 1024], BF16) for j in range(2)]
        gbuf = S3("p3_gbuf", [128, 1024], BF16)
        grbuf = [S3(f"p3_grb{j}", [128, 512], BF16) for j in range(2)]
        gibuf = [S3(f"p3_gib{j}", [128, 512], BF16) for j in range(2)]
        xbuf = S3("p3_xbuf", [128, 1024], BF16)
        ybuf = S3("p3_ybuf", [128, 1024], BF16)
        hbuf = [S3(f"p3_hbuf{j}", [128, 1024], BF16) for j in range(2)]
        hrb = [S3(f"p3_hrb{j}", [128, 512], BF16) for j in range(2)]
        hib = [S3(f"p3_hib{j}", [128, 512], BF16) for j in range(2)]
        yob = S3("p3_yob", [64, 512], BF16)
        tw1 = S3("p3_tw1", [128, 512], BF16)
        tw2 = S3("p3_tw2", [128, 512], BF16)

        GB = (PS[:, 0:512], PS[:, 512:1024])
        XB = (PS[:, 1024:1536], PS[:, 1536:2048])
        HB = (PS[:, 2048:2560], PS[:, 2560:3072])
        YB = bass.AP(cx.PS, 3072, [[4096, 64], [1, 512]])

        sm2 = [cx.newsem(f"p3m2{j}") for j in range(2)]
        skb = [cx.newsem(f"p3kb{j}") for j in range(2)]
        syo = cx.newsem("p3yo")

        @block.vector
        def _(v):
            vo, vw = mkvo(cx, v)
            quiesce(cx, v)
            for j in range(2):
                vw(); vo(v.memset(m2b[j][:], 0.0))
        cv_z = cv.val

        ct0, ca0, cv0 = ct.val, ca.val, cv.val
        s1_ct = {}; s2_ct = {}; i1_ct = {}; i2_ct = {}
        gev_ca = {}; xev_ca = {}; hev_ca = {}; yev_ca = {}
        a_cv = {}; b_cv = {}; c_cv = {}
        vt, va, vv = ct0, ca0, cv0
        for it in range(NB + 2):
            if it < NB:       vt += 1; s1_ct[it] = vt
            if 1 <= it <= NB: vt += 1; s2_ct[it - 1] = vt
            if 1 <= it <= NB: vt += 1; i1_ct[it - 1] = vt
            if it >= 2:       vt += 1; i2_ct[it - 2] = vt
            if it < NB:       va += 2; gev_ca[it] = va
            if 1 <= it <= NB: va += 2; xev_ca[it - 1] = va
            if 1 <= it <= NB: va += 2; hev_ca[it - 1] = va
            if it >= 2:       va += 1; yev_ca[it - 2] = va
            if it < NB:       vv += 6; a_cv[it] = vv
            if 1 <= it <= NB: vv += 6; b_cv[it - 1] = vv
            if it >= 2:       vv += 6; c_cv[it - 2] = vv
        sm2_cnt = lambda b: 16 * (b // 2 + 1)
        skb_cnt = lambda b: 64 * (b // 2 + 1)
        syo_cnt = lambda b: 16 * (b + 1)
        sgq = cx.p0b_done["sg"]

        @block.sync
        def _(sync):
            sync.wait_ge(cv.h, cv_z)
            sync.wait_ge(sg.h, sgq)
            for b in range(NB):
                if b >= 2:
                    sync.wait_ge(ct.h, s1_ct[b - 2])
                src = ap(D["dB"], b * 8 * L, [(64, 64), (2 * L, 4), (L, 2), (1, 64)])
                sync.dma_start(out=ap(m2b[b % 2], 0, [(512, 64), (128, 4), (64, 2), (1, 64)]),
                               in_=src).then_inc(sm2[b % 2].h, 16)
                sm2[b % 2].val += 16

        def kb_load(g, bt):
            ch0 = bt * 8
            for ri in range(2):
                for par in range(2):
                    src = ap(D["dKLOC"], (ch0 + par) * 16384 + ri * 128,
                             [(256, 64), (2 * 16384, 4), (1, 128)])
                    dst = ap(kb[bt % 2], par * 64 * 1024 + ri * 512,
                             [(1024, 64), (128, 4), (1, 128)])
                    g.dma_start(out=dst, in_=src).then_inc(skb[bt % 2].h, 16)
                    skb[bt % 2].val += 16

        @block.gpsimd
        def _(g):
            g.wait_ge(cx.p2_sgk.h, cx.p2_sgk.val)
            kb_load(g, 0)
            kb_load(g, 1)

        for it in range(NB + 2):
            @block.tensor
            def _(t, it=it):
                if it < NB:
                    t.wait_ge(sm2[it % 2].h, sm2_cnt(it))
                    if it >= 1:
                        t.wait_ge(ca.h, gev_ca[it - 1])
                    ins = None
                    for pr in range(4):
                        og = GB[pr // 2][:, (pr % 2) * 256:(pr % 2) * 256 + 256]
                        ins = t.matmul(og, m2b[it % 2][:, pr * 128: pr * 128 + 128],
                                       cx.cst["cf"][:], start=(pr % 2 == 0), stop=(pr % 2 == 1))
                    ins.then_inc(ct.h, 1); ct.val += 1
                if 1 <= it <= NB:
                    bt = it - 1
                    t.wait_ge(cv.h, a_cv[bt])
                    if bt >= 1:
                        t.wait_ge(ca.h, xev_ca[bt - 1])
                    seq = [("s2wr", grbuf[bt % 2], 0, "start"), ("s2win", gibuf[bt % 2], 0, "stop"),
                           ("s2wi", grbuf[bt % 2], 1, "start"), ("s2wr", gibuf[bt % 2], 1, "stop")]
                    ins = None
                    for cname, srcb, xbank, role in seq:
                        for sl in range(2):
                            ins = t.matmul(XB[xbank][:, sl * 256:sl * 256 + 256],
                                           cx.cst[cname][:], srcb[:, sl * 256:sl * 256 + 256],
                                           start=(role == "start" and sl == 0),
                                           stop=(role == "stop" and sl == 1))
                    ins.then_inc(ct.h, 1); ct.val += 1
                if 1 <= it <= NB:
                    bt = it - 1
                    t.wait_ge(cv.h, b_cv[bt])
                    if bt >= 1:
                        t.wait_ge(ca.h, hev_ca[bt - 1])
                    ins = None
                    for pr in range(4):
                        og = HB[pr // 2][:, (pr % 2) * 256:(pr % 2) * 256 + 256]
                        t.matmul(og, ybuf[:, pr * 128:pr * 128 + 128], cx.cst["i1a"][:],
                                 start=(pr % 2 == 0), stop=False)
                        ins = t.matmul(og, ybuf[:, 512 + pr * 128:512 + pr * 128 + 128],
                                       cx.cst["i1b"][:], start=False, stop=(pr % 2 == 1))
                    ins.then_inc(ct.h, 1); ct.val += 1
                if it >= 2:
                    bt = it - 2
                    t.wait_ge(cv.h, c_cv[bt])
                    if bt >= 1:
                        t.wait_ge(ca.h, yev_ca[bt - 1])
                    t.matmul(YB, cx.cst["i2r"][:], hrb[bt % 2][:], start=True, stop=False)
                    ins = t.matmul(YB, cx.cst["i2in"][:], hib[bt % 2][:], start=False, stop=True)
                    ins.then_inc(ct.h, 1); ct.val += 1
            @block.scalar
            def _(s, it=it):
                ao, aw = mkao(cx, s)
                if it < NB:
                    s.wait_ge(ct.h, s1_ct[it])
                    if it >= 1:
                        s.wait_ge(cv.h, a_cv[it - 1])
                    aw(); ao(s.activation(gbuf[:, 0:512], GB[0][:], AF.Copy))
                    aw(); ao(s.activation(gbuf[:, 512:1024], GB[1][:], AF.Copy))
                if 1 <= it <= NB:
                    bt = it - 1
                    s.wait_ge(ct.h, s2_ct[bt])
                    if bt >= 1:
                        s.wait_ge(cv.h, b_cv[bt - 1])
                    aw(); ao(s.activation(xbuf[:, 0:512], XB[0][:], AF.Copy))
                    aw(); ao(s.activation(xbuf[:, 512:1024], XB[1][:], AF.Copy))
                if 1 <= it <= NB:
                    bt = it - 1
                    s.wait_ge(ct.h, i1_ct[bt])
                    if bt >= 2:
                        s.wait_ge(cv.h, c_cv[bt - 2])
                    aw(); ao(s.activation(hbuf[bt % 2][:, 0:512], HB[0][:], AF.Copy))
                    aw(); ao(s.activation(hbuf[bt % 2][:, 512:1024], HB[1][:], AF.Copy))
                if it >= 2:
                    bt = it - 2
                    s.wait_ge(ct.h, i2_ct[bt])
                    if bt >= 1:
                        s.wait_ge(syo.h, syo_cnt(bt - 1))
                    aw(); ao(s.activation(yob[:], YB, AF.Copy))
            @block.vector
            def _(v, it=it):
                vo, vw = mkvo(cx, v)
                if it < NB:
                    v.wait_ge(ca.h, gev_ca[it])
                    if it >= 2:
                        v.wait_ge(ct.h, s2_ct[it - 2])
                    grv = ap(gbuf, 0, [(1024, 128), (256, 4), (1, 128)])
                    giv = ap(gbuf, 128, [(1024, 128), (256, 4), (1, 128)])
                    t1rb = ap(cx.cst["t1r"], 0, [(128, 128), (0, 4), (1, 128)])
                    t1ib = ap(cx.cst["t1i"], 0, [(128, 128), (0, 4), (1, 128)])
                    vw(); vo(v.tensor_tensor(tw1[:], grv, t1rb, ALU.mult))
                    vw(); vo(v.tensor_tensor(tw2[:], giv, t1ib, ALU.mult))
                    vw(); vo(v.tensor_tensor(grbuf[it % 2][:], tw1[:], tw2[:], ALU.subtract))
                    vw(); vo(v.tensor_tensor(tw1[:], grv, t1ib, ALU.mult))
                    vw(); vo(v.tensor_tensor(tw2[:], giv, t1rb, ALU.mult))
                    vw(); vo(v.tensor_tensor(gibuf[it % 2][:], tw1[:], tw2[:], ALU.add))
                if 1 <= it <= NB:
                    bt = it - 1
                    v.wait_ge(ca.h, xev_ca[bt])
                    v.wait_ge(skb[bt % 2].h, skb_cnt(bt))
                    if bt >= 1:
                        v.wait_ge(ct.h, i1_ct[bt - 1])
                    xr = xbuf[:, 0:512]; xi = xbuf[:, 512:1024]
                    kr = kb[bt % 2][:, 0:512]; ki = kb[bt % 2][:, 512:1024]
                    vw(); vo(v.tensor_tensor(tw1[:], xr, kr, ALU.mult))
                    vw(); vo(v.tensor_tensor(tw2[:], xi, ki, ALU.mult))
                    vw(); vo(v.tensor_tensor(ybuf[:, 0:512], tw1[:], tw2[:], ALU.subtract))
                    vw(); vo(v.tensor_tensor(tw1[:], xr, ki, ALU.mult))
                    vw(); vo(v.tensor_tensor(tw2[:], xi, kr, ALU.mult))
                    vw(); vo(v.tensor_tensor(ybuf[:, 512:1024], tw1[:], tw2[:], ALU.add))
                if it >= 2:
                    bt = it - 2
                    v.wait_ge(ca.h, hev_ca[bt])
                    if bt >= 2:
                        v.wait_ge(ct.h, i2_ct[bt - 2])
                    hrv = ap(hbuf[bt % 2], 0, [(1024, 128), (256, 4), (128, 2), (1, 64)])
                    hiv = ap(hbuf[bt % 2], 64, [(1024, 128), (256, 4), (128, 2), (1, 64)])
                    t2rb = ap(cx.cst["t2r"], 0, [(128, 128), (0, 8), (1, 64)])
                    t2ib = ap(cx.cst["t2i"], 0, [(128, 128), (0, 8), (1, 64)])
                    vw(); vo(v.tensor_tensor(tw1[:], hrv, t2rb, ALU.mult))
                    vw(); vo(v.tensor_tensor(tw2[:], hiv, t2ib, ALU.mult))
                    vw(); vo(v.tensor_tensor(hrb[bt % 2][:], tw1[:], tw2[:], ALU.subtract))
                    vw(); vo(v.tensor_tensor(tw1[:], hrv, t2ib, ALU.mult))
                    vw(); vo(v.tensor_tensor(tw2[:], hiv, t2rb, ALU.mult))
                    vw(); vo(v.tensor_tensor(hib[bt % 2][:], tw1[:], tw2[:], ALU.add))
            @block.gpsimd
            def _(g, it=it):
                if it >= 1 and it + 1 <= NB - 1:
                    g.wait_ge(cv.h, b_cv[it - 1])
                    kb_load(g, it + 1)
                if it >= 2:
                    bt = it - 2
                    g.wait_ge(ca.h, yev_ca[bt])
                    dst = ap(D["dY"], bt * 8 * L, [(64, 64), (2 * L, 4), (L, 2), (1, 64)])
                    src = ap(yob, 0, [(512, 64), (128, 4), (64, 2), (1, 64)])
                    g.dma_start(out=dst, in_=src).then_inc(syo.h, 16)
                    syo.val += 16
        assert ct.val == vt
        assert ca.val == va
        assert cv.val == vv
    cx.p3_done = True


def emit_p4(cx):
    nc, block, I, D = cx.nc, cx.block, cx.I, cx.D
    sd, sg, sa, cv, ca, ct = cx.sd, cx.sg, cx.sa, cx.cv, cx.ca, cx.ct
    PS = cx.PS
    snap = [(x, x.val) for x in cx.all_sems]

    p4 = contextlib.ExitStack()
    with p4:
        S4 = lambda nm, shp, dt=F32: p4.enter_context(nc.sbuf_tensor(nm, shp, dt))
        wo = S4("p4_wo", [128, 16384], BF16)
        ybf = [S4(f"p4_y{j}", [128, 4096], BF16) for j in range(2)]
        ygb = S4("p4_yg", [128, 4096], BF16)
        u = S4("p4_u", [128, 512], BF16); s2b = S4("p4_s2", [128, 512], BF16)
        a1 = S4("p4_a1", [128, 512], BF16); th = S4("p4_th", [128, 512], BF16)
        zb = S4("p4_z", [128, 512], BF16); h1b = S4("p4_h1", [128, 512], BF16)
        sgv = S4("p4_sg", [128, 512])
        atile = S4("p4_at", [128, 512], BF16)

        swo = cx.newsem("p4wo")
        syl = [cx.newsem(f"p4y{j}") for j in range(2)]
        sap = cx.newsem("p4ap")

        @block.gpsimd
        def _(g):
            quiesce(cx, g)
            src = ap(I["wout"], 0, [(2 * H, 128), (128 * 2 * H, 8), (1, 2 * H)])
            g.dma_start(out=ap(wo, 0, [(16384, 128), (2 * H, 8), (1, 2 * H)]),
                        in_=src).then_inc(swo.h, 16)
            swo.val += 16

        DV4 = 8 * 7 + 8
        DA4 = 8 + 8
        DPE4 = 16
        ct0 = ct.val
        ca0 = ca.val
        cv0 = cv.val

        @block.sync
        def _(sync):
            quiesce(cx, sync)
            for tb in range(8):
                if tb >= 2:
                    sync.wait_ge(cv.h, cv0 + (tb - 2) * DV4 + 8 * 7)
                src = ap(D["dY"], tb * 512, [(L, 128), (128 * L, 8), (1, 512)])
                dst = ap(ybf[tb % 2], 0, [(4096, 128), (512, 8), (1, 512)])
                sync.dma_start(out=dst, in_=src).then_inc(syl[tb % 2].h, 16)
                syl[tb % 2].val += 16

        for tb in range(8):
            v0 = cv0 + tb * DV4
            a0 = ca0 + tb * DA4
            pe0 = ct0 + tb * DPE4
            @block.vector
            def _(v):
                vo, vw = mkvo(cx, v)
                if tb == 0:
                    quiesce(cx, v, snap)
                v.wait_ge(syl[tb % 2].h, 16 * (tb // 2 + 1))
                if tb > 0:
                    v.wait_ge(ct.h, pe0)
                for hc in range(8):
                    xnsl = cx.xnF[:, hc * L + tb * 512: hc * L + tb * 512 + 512]
                    ysl = ybf[tb % 2][:, hc * 512:hc * 512 + 512]
                    dcol = cx.sm["d_t"][:, hc:hc + 1]
                    vw(); vo(v.scalar_tensor_tensor(u[:], xnsl, dcol, ysl, ALU.mult, ALU.add))
                    vw(); vo(v.tensor_tensor(s2b[:], u[:], u[:], ALU.mult))
                    vw(); vo(v.tensor_scalar(s2b[:], s2b[:], GELU_A, 1.0, ALU.mult, ALU.add))
                    vw(); vo(v.tensor_tensor(a1[:], u[:], s2b[:], ALU.mult))
                    v.wait_ge(ca.h, a0 + hc + 1)
                    vw(); vo(v.tensor_tensor(zb[:], u[:], th[:], ALU.mult))
                    vw(); vo(v.tensor_scalar(h1b[:], u[:], 0.5, None, ALU.mult))
                    vw(); vo(v.scalar_tensor_tensor(ygb[:, hc * 512:hc * 512 + 512],
                                                    zb[:], 0.5, h1b[:], ALU.mult, ALU.add))
            @block.scalar
            def _(s):
                ao, aw = mkao(cx, s)
                if tb == 0:
                    quiesce(cx, s, snap)
                for hc in range(8):
                    s.wait_ge(cv.h, v0 + hc * 7 + 4)
                    if hc > 0:
                        s.wait_ge(cv.h, v0 + (hc - 1) * 7 + 5)
                    elif tb > 0:
                        s.wait_ge(cv.h, v0 - DV4 + 7 * 7 + 5)
                    aw(); ao(s.activation(th[:], a1[:], AF.Tanh, scale=GELU_C))
            @block.tensor
            def _(t):
                t.wait_ge(swo.h, 16)
                t.wait_ge(cv.h, v0 + 8 * 7)
                for pair in range(8):
                    for half in range(2):
                        oc = pair + 8 * half
                        bank = PS[:, ((2 * pair + half) % 4) * 512:((2 * pair + half) % 4) * 512 + 512]
                        if tb > 0 or pair >= 2:
                            prev = (tb * 8 + pair - 2)
                            t.wait_ge(ca.h, ca0 + (prev // 8) * DA4 + 8 + (prev % 8) + 1)
                            t.wait_ge(cv.h, cv0 + (prev // 8) * DV4 + 56 + (prev % 8) + 1)
                        ins = None
                        for hc in range(8):
                            ins = t.matmul(bank, wo[:, hc * 2048 + oc * 128: hc * 2048 + oc * 128 + 128],
                                           ygb[:, hc * 512:hc * 512 + 512],
                                           start=(hc == 0), stop=(hc == 7))
                        ins.then_inc(ct.h, 1)
                        ct.val += 1
            @block.scalar
            def _(s):
                ao, aw = mkao(cx, s)
                for pair in range(8):
                    s.wait_ge(ct.h, pe0 + 2 * pair + 2)
                    if pair > 0 or tb > 0:
                        prevp = tb * 8 + pair - 1
                        s.wait_ge(cv.h, cv0 + (prevp // 8) * DV4 + 56 + (prevp % 8) + 1)
                    gb = PS[:, ((2 * pair + 1) % 4) * 512:((2 * pair + 1) % 4) * 512 + 512]
                    aw(); ao(s.activation(sgv[:], gb, AF.Sigmoid,
                                          bias=cx.sm["bout_t"][:, pair + 8:pair + 9], scale=1.0))
            @block.vector
            def _(v):
                vo, vw = mkvo(cx, v)
                for pair in range(8):
                    v.wait_ge(ct.h, pe0 + 2 * pair + 1)
                    v.wait_ge(ca.h, a0 + 8 + pair + 1)
                    if pair > 0 or tb > 0:
                        prevp = tb * 8 + pair - 1
                        v.wait_ge(sap.h, 16 * (prevp + 1))
                    ab = PS[:, ((2 * pair) % 4) * 512:((2 * pair) % 4) * 512 + 512]
                    vw(); vo(v.scalar_tensor_tensor(atile[:], ab, cx.sm["bout_t"][:, pair:pair + 1],
                                                    sgv[:], ALU.add, ALU.mult))
            @block.gpsimd
            def _(g):
                for pair in range(8):
                    g.wait_ge(cv.h, v0 + 56 + pair + 1)
                    dst = ap(D["dP"], (pair * 128) * L + tb * 512, [(L, 128), (1, 512)])
                    g.dma_start(out=dst, in_=atile[:]).then_inc(sap.h, 16)
                    sap.val += 16
            if tb % 2 == 1:
                p5_chunk(cx, cx.st5, 28 + tb // 2, qsnap=snap if tb == 1 else None)
    p5_close(cx, cx.st5)
    cx.p4_done = True


def p5_setup(cx):
    nc, block, I, D = cx.nc, cx.block, cx.I, cx.D
    es = contextlib.ExitStack()
    S5 = lambda nm, shp, dt=F32: es.enter_context(nc.sbuf_tensor(nm, shp, dt))
    st = {"es": es}
    st["w1b"] = [S5(f"p5_w1{j}", [128, 4096], BF16) for j in range(2)]
    st["w3b"] = [S5(f"p5_w3{j}", [128, 4096], BF16) for j in range(2)]
    st["s1b"] = [S5(f"p5_s1{j}", [128, 512], BF16) for j in range(2)]
    st["h3b"] = [S5(f"p5_h3{j}", [128, 512], BF16) for j in range(2)]
    st["sw"] = [cx.newsem(f"p5w{j}") for j in range(2)]
    st["sht"] = [cx.newsem("p5ht0"), cx.newsem("p5ht1")]
    st["ct5"] = cx.newsem("ct5")
    st["ca5"] = cx.newsem("ca5")
    st["swcnt"] = {}
    st["snap"] = [(x, x.val) for x in cx.all_sems]

    def wload(g, u):
        sw = st["sw"][u % 2]
        if u < 16:
            for (w, t) in ((I["we1"], st["w1b"][u % 2]), (I["we3"], st["w3b"][u % 2])):
                src = ap(w, u * H * INTER, [(INTER, 128), (128 * INTER, 8), (1, INTER)])
                g.dma_start(out=ap(t, 0, [(4096, 128), (512, 8), (1, 512)]),
                            in_=src).then_inc(sw.h, 16)
                sw.val += 16
        else:
            hf = u - 16
            for (w, t) in ((I["ws1"], st["w1b"][u % 2]), (I["ws3"], st["w3b"][u % 2])):
                src = ap(w, hf * 512, [(H, 128), (128 * H, 8), (1, 512)])
                g.dma_start(out=ap(t, 0, [(4096, 128), (512, 8), (1, 512)]),
                            in_=src).then_inc(sw.h, 16)
                sw.val += 16
        st["swcnt"][u] = sw.val
    st["wload"] = wload

    @block.gpsimd
    def _(g):
        quiesce(cx, g)
        wload(g, 0)
        wload(g, 1)
    return st


def p5_chunk(cx, st, ci, qsnap=None):
    block, PS = cx.block, cx.PS
    ct5, ca5 = st["ct5"], st["ca5"]
    sw, sht = st["sw"], st["sht"]
    w1b, w3b = st["w1b"], st["w3b"]
    s1b, h3b = st["s1b"], st["h3b"]
    lo, hi = 18 * ci, 18 * ci + 18

    @block.tensor
    def _(t):
        if qsnap is not None:
            quiesce(cx, t, qsnap)
        for n in range(lo, hi):
            u, k = n // 32, n % 32
            ic, ts = k // 8, k % 8
            if n == 0:
                quiesce(cx, t, st["snap"])
            if k == 0:
                t.wait_ge(sw[u % 2].h, st["swcnt"][u])
            pair = n % 2
            b1 = PS[:, (4 + 2 * pair) * 512:(4 + 2 * pair) * 512 + 512]
            b3 = PS[:, (5 + 2 * pair) * 512:(5 + 2 * pair) * 512 + 512]
            if n >= 2:
                t.wait_ge(ca5.h, 2 * (n - 2) + 2)
            ins = None
            for dc in range(8):
                ins = t.matmul(b1, w1b[u % 2][:, dc * 512 + ic * 128: dc * 512 + ic * 128 + 128],
                               cx.xnF[:, dc * L + ts * 512: dc * L + ts * 512 + 512],
                               start=(dc == 0), stop=(dc == 7))
            ins.then_inc(ct5.h, 1); ct5.val += 1
            for dc in range(8):
                ins = t.matmul(b3, w3b[u % 2][:, dc * 512 + ic * 128: dc * 512 + ic * 128 + 128],
                               cx.xnF[:, dc * L + ts * 512: dc * L + ts * 512 + 512],
                               start=(dc == 0), stop=(dc == 7))
            ins.then_inc(ct5.h, 1); ct5.val += 1

    @block.scalar
    def _(s):
        for n in range(lo, hi):
            u = n // 32
            ic = (n % 32) // 8
            pair = n % 2
            b1 = PS[:, (4 + 2 * pair) * 512:(4 + 2 * pair) * 512 + 512]
            s.wait_ge(ct5.h, 2 * n + 1)
            if SELF_WAITS and ca5.val > 0:
                s.wait_ge(ca5.h, ca5.val)
            if n >= 2:
                s.wait_ge(sht[n % 2].h, 32 * (n // 2))
            bcol = (cx.sm["be1_t"][:, u * 4 + ic:u * 4 + ic + 1] if u < 16
                    else cx.sm["bs1_t"][:, (u - 16) * 4 + ic:(u - 16) * 4 + ic + 1])
            ins = s.activation(s1b[n % 2][:], b1, AF.Silu, bias=bcol, scale=1.0)
            ins.then_inc(ca5.h, 1); ca5.val += 1
            b3 = PS[:, (5 + 2 * pair) * 512:(5 + 2 * pair) * 512 + 512]
            s.wait_ge(ct5.h, 2 * n + 2)
            ins = s.activation(h3b[n % 2][:], b3, AF.Copy)
            ins.then_inc(ca5.h, 1); ca5.val += 1

    @block.gpsimd
    def _(g):
        for n in range(lo, hi):
            u, k = n // 32, n % 32
            ic, ts = k // 8, k % 8
            g.wait_ge(ca5.h, 2 * n + 2)
            irow = (128 * 64 + (u - 16) * 512 + ic * 128) if u >= 16 else (u * 512 + ic * 128)
            dst = ap(cx.D["dS"], irow * L + ts * 512, [(L, 128), (1, 512)])
            g.dma_start(out=dst, in_=s1b[n % 2][:]).then_inc(sht[n % 2].h, 16)
            sht[n % 2].val += 16
            dst = ap(cx.D["dH"], irow * L + ts * 512, [(L, 128), (1, 512)])
            g.dma_start(out=dst, in_=h3b[n % 2][:]).then_inc(sht[n % 2].h, 16)
            sht[n % 2].val += 16
            if k == 31 and u + 2 < 18:
                g.wait_ge(ct5.h, 64 * (u + 1))
                st["wload"](g, u + 2)


def p5_close(cx, st):
    assert st["ct5"].val == 2 * 576
    assert st["ca5"].val == 2 * 576
    st["es"].close()
    cx.p5_done = True


def emit_p6(cx):
    nc, block, I, D = cx.nc, cx.block, cx.I, cx.D
    sd, sg, sa, cv, ca, ct = cx.sd, cx.sg, cx.sa, cx.cv, cx.ca, cx.ct
    PS = cx.PS
    snap = [(x, x.val) for x in cx.all_sems]
    cx.xnF_cm.__exit__(None, None, None)
    cx.xnF_freed = True

    p6 = contextlib.ExitStack()
    with p6:
        S6 = lambda nm, shp, dt=F32: p6.enter_context(nc.sbuf_tensor(nm, shp, dt))
        w2a = [S6(f"p6_w2{e}", [128, 4096], BF16) for e in range(16)]
        ws2b = S6("p6_ws2", [128, 8192], BF16)
        bep = S6("p6_bep", [128, 1024], BF16)
        hstr = [S6(f"p6_h{j}", [128, 2048], BF16) for j in range(4)]
        sstr = [S6(f"p6_s{j}", [128, 2048], BF16) for j in range(4)]
        abuf = S6("p6_a", [128, 2048], BF16)
        cmb6 = S6("p6_cmb", [128, 4096], BF16)
        mtile = [S6("p6_m", [128, 2048], BF16)] * 2

        sw6 = cx.newsem("p6w")
        sal = cx.newsem("p6a")
        smo = cx.newsem("p6mo")
        scm6 = cx.newsem("p6cm")

        @block.vector
        def _(v):
            vo, vw = mkvo(cx, v)
            quiesce(cx, v)
            vw(); vo(v.memset(bep[:], 0.0))
        cv_bz = cv.val

        @block.gpsimd
        def _(g):
            quiesce(cx, g)
            g.wait_ge(cv.h, cv_bz)
            g.dma_start(out=bep[0:16, :], in_=I["be2"].ap()).then_inc(sw6.h, 16)
            sw6.val += 16
            for e in range(16):
                g.wait_ge(sw6.h, sw6.val)
                src = ap(I["we2"], e * INTER * H, [(H, 128), (128 * H, 4), (1, H)])
                g.dma_start(out=ap(w2a[e], 0, [(4096, 128), (H, 4), (1, H)]),
                            in_=src).then_inc(sw6.h, 16)
                sw6.val += 16
            g.wait_ge(sw6.h, sw6.val)
            src = ap(I["ws2"], 0, [(H, 128), (128 * H, 8), (1, H)])
            g.dma_start(out=ap(ws2b, 0, [(8192, 128), (H, 8), (1, H)]),
                        in_=src).then_inc(sw6.h, 16)
            sw6.val += 16

        NTS = 16
        ct0 = ct.val
        ca0 = ca.val
        cv0 = cv.val

        VV6 = 16 * 5 + 8 + 8
        mm_ct = lambda ts, e: ct0 + ts * 17 + e + 1
        prod_cv = lambda ts, e: cv0 + ts * VV6 + (5 * (e + 1) if e < 16 else 88)
        evac_cv = lambda ts: cv0 + ts * VV6 + VV6

        she = [cx.newsem(f"p6he{j}") for j in range(4)]
        CNT4 = [5, 4, 4, 4]

        def she_cnt(ts, e):
            return 32 * (ts * CNT4[e % 4] + e // 4 + 1)

        def she_load(g, ts):
            for e in range(17):
                if e >= 4:
                    tsP, eP = ts, e - 4
                elif ts > 0:
                    tsP, eP = ts - 1, (16 if e % 4 == 0 else 12 + e % 4)
                else:
                    tsP = None
                if tsP is not None:
                    g.wait_ge(ct.h, mm_ct(tsP, eP))
                    g.wait_ge(she[e % 4].h, she_cnt(ts, e) - 32)
                nic = 8 if e == 16 else 4
                irow = 128 * 64 if e == 16 else e * 512
                for dn, buf in (("dS", sstr[e % 4]), ("dH", hstr[e % 4])):
                    src = ap(D[dn], irow * L + ts * 256, [(L, 128), (128 * L, nic), (1, 256)])
                    dst = ap(buf, 0, [(2048, 128), (256, nic), (1, 256)])
                    g.dma_start(out=dst, in_=src).then_inc(she[e % 4].h, 16)
                    she[e % 4].val += 16

        def cm6load(g, ts):
            if ts >= 1:
                g.wait_ge(cv.h, prod_cv(ts - 1, 15))
            src = ap(D["dCMBT"], ts * 256, [(0, 128), (L, 16), (1, 256)])
            g.dma_start(out=ap(cmb6, 0, [(4096, 128), (256, 16), (1, 256)]),
                        in_=src).then_inc(scm6.h, 16)
            scm6.val += 16

        def abuf_load(g, ts):
            if ts >= 1:
                g.wait_ge(cv.h, evac_cv(ts - 1))
            src = ap(D["dP"], ts * 256, [(L, 128), (128 * L, 8), (1, 256)])
            dst = ap(abuf, 0, [(2048, 128), (256, 8), (1, 256)])
            g.dma_start(out=dst, in_=src).then_inc(sal.h, 16)
            sal.val += 16

        @block.sync
        def _(sync):
            quiesce(cx, sync, snap)
            abuf_load(sync, 0)
            cm6load(sync, 0)
            she_load(sync, 0)

        for ts in range(NTS):
            @block.tensor
            def _(t):
                if ts == 0:
                    quiesce(cx, t, snap)
                    t.wait_ge(cv.h, cv_bz)
                    t.wait_ge(sw6.h, 16)
                if ts > 1:
                    t.wait_ge(cv.h, evac_cv(ts - 2))
                boff = (ts % 2) * 2048
                for dc in range(8):
                    og = PS[:, boff + dc * 256:boff + dc * 256 + 256]
                    t.matmul(og, bep[:, dc * 128:dc * 128 + 128],
                             cx.cmbT[:, ts * 256:ts * 256 + 256],
                             start=(dc % 2 == 0), stop=False)
                for e in range(17):
                    if ts == 0:
                        t.wait_ge(sw6.h, 16 * (e + 2))
                    t.wait_ge(cv.h, prod_cv(ts, e))
                    nic = 8 if e == 16 else 4
                    buf = hstr[e % 4]
                    wsrc = ws2b if e == 16 else w2a[e]
                    ins = None
                    for ic in range(nic):
                        for dc in range(8):
                            og = PS[:, boff + dc * 256:boff + dc * 256 + 256]
                            ins = t.matmul(og, wsrc[:, ic * H + dc * 128: ic * H + dc * 128 + 128],
                                           buf[:, ic * 256:ic * 256 + 256],
                                           start=False,
                                           stop=(e == 16 and ic == nic - 1 and dc % 2 == 1))
                    ins.then_inc(ct.h, 1)
                    ct.val += 1
            @block.vector
            def _(v):
                vo, vw = mkvo(cx, v)
                for e in range(17):
                    v.wait_ge(she[e % 4].h, she_cnt(ts, e))
                    if e == 0:
                        v.wait_ge(scm6.h, 16 * (ts + 1))
                    nic = 8 if e == 16 else 4
                    for ic in range(nic):
                        bcol = (cx.sm["be3_t"][:, e * 4 + ic:e * 4 + ic + 1] if e < 16
                                else cx.sm["bs3_t"][:, ic:ic + 1])
                        sl = slice(ic * 256, ic * 256 + 256)
                        vw(); vo(v.scalar_tensor_tensor(hstr[e % 4][:, sl], hstr[e % 4][:, sl],
                                                        bcol[:, 0:1], sstr[e % 4][:, sl],
                                                        ALU.add, ALU.mult))
                    if e < 16:
                        hv = ap(hstr[e % 4], 0, [(2048, 128), (256, 4), (1, 256)])
                        cmv = ap(cmb6, e * 256, [(4096, 128), (0, 4), (1, 256)])
                        vw(); vo(v.tensor_tensor(hv, hv, cmv, ALU.mult))
                assert cv.val == prod_cv(ts, 16)
                v.wait_ge(ct.h, ct0 + (ts + 1) * 17)
                v.wait_ge(sal.h, 16 * (ts + 1))
                if ts > 0:
                    v.wait_ge(smo.h, 16 * ts)
                boff = (ts % 2) * 2048
                for dc in range(8):
                    og = PS[:, boff + dc * 256:boff + dc * 256 + 256]
                    vw(); vo(v.scalar_tensor_tensor(mtile[ts % 2][:, dc * 256:dc * 256 + 256], og,
                                                    cx.sm["bs2_t"][:, dc:dc + 1],
                                                    abuf[:, dc * 256:dc * 256 + 256],
                                                    ALU.add, ALU.add))
                assert cv.val == evac_cv(ts)
            @block.sync
            def _(sync):
                if ts + 1 < NTS:
                    cm6load(sync, ts + 1)
                    she_load(sync, ts + 1)
                    abuf_load(sync, ts + 1)
            @block.gpsimd
            def _(g):
                g.wait_ge(cv.h, evac_cv(ts))
                dst = ap(D["dM"], ts * 256, [(L, 128), (128 * L, 8), (1, 256)])
                src = ap(mtile[ts % 2], 0, [(2048, 128), (256, 8), (1, 256)])
                g.dma_start(out=dst, in_=src).then_inc(smo.h, 16)
                smo.val += 16
    cx.p6_done = True


def emit_p7(cx):
    nc, block, I, D = cx.nc, cx.block, cx.I, cx.D
    sd, sa, cv = cx.sd, cx.sa, cx.cv
    snap = [(x, x.val) for x in cx.all_sems]

    p7 = contextlib.ExitStack()
    with p7:
        S7 = lambda nm, shp, dt=F32: p7.enter_context(nc.sbuf_tensor(nm, shp, dt))
        xb = [S7(f"p7_x{j}", [128, H]) for j in range(2)]
        mb = [S7(f"p7_m{j}", [128, H], BF16) for j in range(2)]
        ob = [S7(f"p7_o{j}", [128, H]) for j in range(2)]
        sx = [cx.newsem(f"p7x{j}") for j in range(2)]
        smt = [cx.newsem(f"p7m{j}") for j in range(2)]
        sot = [cx.newsem("p7o0"), cx.newsem("p7o1")]

        cv0 = cv.val
        def store_out(sync, j):
            sync.wait_ge(cv.h, cv0 + j + 1)
            sync.dma_start(out=cx.out.ap()[j * 128:(j + 1) * 128, :],
                           in_=ob[j % 2][:]).then_inc(sot[j % 2].h, 16)
            sot[j % 2].val += 16

        @block.sync
        def _(sync):
            quiesce(cx, sync, snap)
            for i in range(32):
                if i >= 2:
                    store_out(sync, i - 2)
                sync.dma_start(out=xb[i % 2][:], in_=I["x"].ap()[i * 128:(i + 1) * 128, :]
                               ).then_inc(sx[i % 2].h, 16)
                sx[i % 2].val += 16
            store_out(sync, 30)
            store_out(sync, 31)
            sync.wait_ge(sot[0].h, sot[0].val)
            sync.wait_ge(sot[1].h, sot[1].val)

        @block.scalar
        def _(s):
            quiesce(cx, s, snap)
            for i in range(32):
                if i >= 2:
                    s.wait_ge(cv.h, cv0 + (i - 2) + 1)
                src = ap(D["dM"], i * 128, [(L, H), (1, 128)])
                s.dma_start(out=mb[i % 2][:], in_=src, transpose=True
                            ).then_inc(smt[i % 2].h, 16)
                smt[i % 2].val += 16

        @block.vector
        def _(v):
            vo, vw = mkvo(cx, v)
            for i in range(32):
                v.wait_ge(sx[i % 2].h, 16 * (i // 2 + 1))
                v.wait_ge(smt[i % 2].h, 16 * (i // 2 + 1))
                if i >= 2:
                    v.wait_ge(sot[i % 2].h, 16 * ((i - 2) // 2 + 1))
                vw(); vo(v.tensor_tensor(ob[i % 2][:], xb[i % 2][:], mb[i % 2][:], ALU.add))


    cx.p7_done = True


def emit_taps(cx):
    if not cx.taps:
        return
    block = cx.block
    sd, sg = cx.sd, cx.sg

    @block.sync
    def _(sync):
        quiesce(cx, sync)
        for nm, t in cx.taps.items():
            if nm.split("__")[0] not in cx.D:
                continue
            src = cx.D[nm.split("__")[0]]
            sync.dma_start(out=t.ap(), in_=src.ap()).then_inc(sd.h, 16)
            sd.val += 16


def emit_fin(cx):
    block = cx.block
    sd = cx.sd

    @block.sync
    def _(sync):
        quiesce(cx, sync)
        if DEBUG["stop_after"] is not None:
            pass


def shard_inputs(inputs):
    gi = lambda k: np.ascontiguousarray(np.asarray(inputs[k]))
    x = gi("L0_attn_x")
    consts = host_consts()
    in_maps = []
    for c in range(NCORE):
        m = {
            "x": np.ascontiguousarray(x[c]),
            "logdt_t": np.ascontiguousarray(gi("L0_attn_log_dt").reshape(1, H)),
            "arl_t": np.ascontiguousarray(gi("L0_attn_A_real_log").T),
            "aim_t": np.ascontiguousarray(gi("L0_attn_A_imag").T),
            "cr_t": np.ascontiguousarray(gi("L0_attn_C_real").T),
            "ci_t": np.ascontiguousarray(gi("L0_attn_C_imag").T),
            "d_t": np.ascontiguousarray(gi("L0_attn_D").reshape(8, 128).T),
            "wout": gi("L0_attn_W_out"),
            "bout_t": np.ascontiguousarray(gi("L0_attn_b_out").reshape(16, 128).T),
            "gw_t": np.ascontiguousarray(gi("L0_ffn_gate_w").T),
            "gb": np.ascontiguousarray(gi("L0_ffn_gate_b").reshape(1, E)),
            "we1": gi("L0_ffn_We1"), "we2": gi("L0_ffn_We2"), "we3": gi("L0_ffn_We3"),
            "be1_t": np.ascontiguousarray(
                gi("L0_ffn_be1").T.reshape(4, 128, E).transpose(1, 2, 0).reshape(128, E * 4)),
            "be3_t": np.ascontiguousarray(
                gi("L0_ffn_be3").T.reshape(4, 128, E).transpose(1, 2, 0).reshape(128, E * 4)),
            "be2": gi("L0_ffn_be2"),
            "ws1": gi("L0_ffn_ws1"), "ws2": gi("L0_ffn_ws2"), "ws3": gi("L0_ffn_ws3"),
            "bs1_t": np.ascontiguousarray(gi("L0_ffn_bs1").reshape(8, 128).T),
            "bs3_t": np.ascontiguousarray(gi("L0_ffn_bs3").reshape(8, 128).T),
            "bs2_t": np.ascontiguousarray(gi("L0_ffn_bs2").reshape(8, 128).T),
        }
        for k, v in consts.items():
            m["c_" + k] = v
        in_maps.append(m)
    return in_maps


def kernel(**inputs):
    in_maps = shard_inputs(inputs)
    nc = build_nc()
    res = run_bass_kernel_spmd(nc, in_maps, core_ids=list(range(NCORE)))
    return np.stack([np.asarray(r["out"], np.float32) for r in res.results], axis=0)


# version 20 (final)
# speedup vs baseline: 1.0213x; 1.0213x over previous
import contextlib
import math
import numpy as np
import ml_dtypes
import concourse.bass as bass
import concourse.mybir as mybir
from concourse.bass_utils import run_bass_kernel_spmd

F32 = mybir.dt.float32
BF16 = mybir.dt.bfloat16
AF = mybir.ActivationFunctionType
ALU = mybir.AluOpType
AX = mybir.AxisListType

B, L, H = 8, 4096, 1024
NCORE = 8
HSL = H // NCORE
NFFT = 8192
E, INTER = 16, 512

TWO_PI = 2 * math.pi
RC1 = float(np.float32(6.28125))
RC2 = float(np.float32(np.float64(TWO_PI) - 6.28125))
MAGIC = 12582912.0
GELU_C = float(np.sqrt(2.0 / np.pi))
GELU_A = 0.044715

DEBUG = {"taps": {}, "stop_after": None}


def _bfc(x):
    return np.asarray(x, dtype=ml_dtypes.bfloat16)


def host_consts():
    a = np.arange(64, dtype=np.float64)[:, None]
    p = np.arange(128, dtype=np.float64)[None, :]
    F128 = np.exp(-2j * np.pi * a * p / 128.0)
    bb = np.arange(64, dtype=np.float64)[:, None]
    tt = np.arange(64, dtype=np.float64)[None, :]
    W64 = np.exp(-2j * np.pi * bb * tt / 64.0)
    T1 = np.exp(-2j * np.pi * np.arange(64)[:, None] * np.arange(128)[None, :] / NFFT)
    W64c = np.exp(2j * np.pi * np.arange(64)[:, None] * np.arange(64)[None, :] / 64.0)
    T2 = np.exp(2j * np.pi * np.arange(128)[:, None] * np.arange(64)[None, :] / NFFT)
    W128c = np.exp(2j * np.pi * np.arange(128)[:, None] * np.arange(64)[None, :] / 128.0) / NFFT

    c = {}
    cf = np.zeros((128, 256))
    cf[0:64, 0:128] = F128.real
    cf[0:64, 128:256] = F128.imag
    c["cf"] = _bfc(cf)

    def blk(m):
        z = np.zeros((128, 128))
        z[0:64, 0:64] = m
        z[64:128, 64:128] = m
        return _bfc(z)
    c["s2wr"] = blk(W64.real)
    c["s2wi"] = blk(W64.imag)
    c["s2win"] = blk(-W64.imag)
    i1a = np.zeros((128, 256))
    i1a[0:64, 0:64] = W64c.real; i1a[0:64, 64:128] = W64c.imag
    i1a[64:128, 128:192] = W64c.real; i1a[64:128, 192:256] = W64c.imag
    i1b = np.zeros((128, 256))
    i1b[0:64, 0:64] = -W64c.imag; i1b[0:64, 64:128] = W64c.real
    i1b[64:128, 128:192] = -W64c.imag; i1b[64:128, 192:256] = W64c.real
    c["i1a"] = _bfc(i1a)
    c["i1b"] = _bfc(i1b)
    c["i2r"] = _bfc(W128c.real)
    c["i2in"] = _bfc(-W128c.imag)
    c["t1r"] = _bfc(np.vstack([T1.real, T1.real]))
    c["t1i"] = _bfc(np.vstack([T1.imag, T1.imag]))
    c["t2r"] = _bfc(np.hstack([T2.real, T2.real]))
    c["t2i"] = _bfc(np.hstack([T2.imag, T2.imag]))
    c["iota"] = np.broadcast_to(np.arange(64, dtype=np.float32), (64, 64)).copy()
    return c


class Ctx:
    pass


def ap(t, off, dims):
    return bass.AP(t, int(off), [[int(a), int(b)] for a, b in dims])


def build_nc():
    nc = bass.Bass("TRN2", target_bir_lowering=False)
    cx = Ctx()
    cx.nc = nc
    es = contextlib.ExitStack()
    cx.es = es

    I = {}
    def inp(name, shape, dt=F32):
        I[name] = nc.dram_tensor(name, list(shape), dt, kind="ExternalInput")
    inp("x", [L, H])
    inp("logdt_t", [1, H]); inp("arl_t", [64, H]); inp("aim_t", [64, H])
    inp("cr_t", [64, H]); inp("ci_t", [64, H])
    inp("d_t", [128, 8]); inp("wout", [H, 2 * H]); inp("bout_t", [128, 16])
    inp("gw_t", [H, 16]); inp("gb", [1, 16])
    inp("we1", [E, H, INTER]); inp("we2", [E, INTER, H]); inp("we3", [E, H, INTER])
    inp("be1_t", [128, E * 4]); inp("be3_t", [128, E * 4]); inp("be2", [E, H])
    inp("ws1", [H, H]); inp("ws2", [H, H]); inp("ws3", [H, H])
    inp("bs1_t", [128, 8]); inp("bs3_t", [128, 8]); inp("bs2_t", [128, 8])
    hc_consts = host_consts()
    for k, v in hc_consts.items():
        inp("c_" + k, v.shape, BF16 if v.dtype == ml_dtypes.bfloat16 else F32)
    cx.I = I
    cx.out = nc.dram_tensor("out", [L, H], F32, kind="ExternalOutput")
    cx.taps = {nm: nc.dram_tensor("tap_" + nm, list(shape), dt, kind="ExternalOutput")
               for nm, (shape, dt) in DEBUG["taps"].items()}

    D = {}
    D["dA"] = nc.dram_tensor("dA", [L, H], BF16)
    D["dA2"] = nc.dram_tensor("dA2", [L, H], BF16)
    D["dB"] = nc.dram_tensor("dB", [H, L], BF16)
    D["dCMB"] = nc.dram_tensor("dCMB", [L, 128], BF16)
    D["dCMBT"] = nc.dram_tensor("dCMBT", [E, L], BF16)
    D["dKLOC"] = nc.dram_tensor("dKLOC", [H, 64 * 256], BF16)
    D["dWP"] = nc.dram_tensor("dWP", [24 * 64, H], F32)
    D["dY"] = nc.dram_tensor("dY", [H, L], BF16)
    D["dP"] = nc.dram_tensor("dP", [H, L], BF16)
    NIR = E * 4 + 8
    D["dH"] = nc.dram_tensor("dH", [NIR * 128, L], BF16)
    D["dS"] = nc.dram_tensor("dS", [NIR * 128, L], BF16)
    D["dF8"] = nc.dram_tensor("dF8", [64, 8 * H * 8], BF16)
    D["dM"] = nc.dram_tensor("dM", [H, L], BF16)
    D["dDT"] = nc.dram_tensor("dDT", [1, H], F32)
    cx.D = D

    with es:
        cx.S = lambda nm, shp, dt=F32: es.enter_context(nc.sbuf_tensor(nm, shp, dt))
        cx.cst = {k: cx.S("s_" + k, list(v.shape), BF16 if v.dtype == ml_dtypes.bfloat16 else F32)
                  for k, v in hc_consts.items()}
        cx.cmbT = cx.S("cmbT", [128, L], BF16)
        cx.sm = {nm: cx.S("sb_" + nm, [128, w], F32)
                 for nm, w in [("d_t", 8), ("bout_t", 16), ("gb", 16),
                               ("be1_t", E * 4), ("be3_t", E * 4),
                               ("bs1_t", 8), ("bs3_t", 8), ("bs2_t", 8)]}
        cx.PS = es.enter_context(nc.psum_tensor("PS", [128, 4096], F32))
        cx.xnF_cm = nc.sbuf_tensor("xnF", [128, 8 * L], BF16)
        cx.xnF = cx.xnF_cm.__enter__()
        cx.xnF_freed = False

        cx.all_sems = []
        class Sem:
            def __init__(self, name):
                self.h = es.enter_context(nc.semaphore(name))
                self.val = 0
                cx.all_sems.append(self)
        cx.newsem = Sem
        for nm in ["sd", "sg", "sa", "cv", "ca", "ct", "cg", "cc"]:
            setattr(cx, nm, Sem(nm))

        cx.block = es.enter_context(nc.Block())
        phases = ["p0", "p0b", "p1", "p2", "p3", "p4", "p6", "p7"]
        stop = DEBUG["stop_after"]
        emit_init(cx)
        for phname in phases:
            globals()["emit_" + phname](cx)
            if stop == phname:
                break
        emit_taps(cx)
        emit_fin(cx)
        if not cx.xnF_freed:
            cx.xnF_cm.__exit__(None, None, None)
    return nc


def quiesce(cx, eng, snap=None):
    if snap is None:
        snap = [(s, s.val) for s in cx.all_sems]
    for s, v in snap:
        if v > 0:
            eng.wait_ge(s.h, v)


SELF_WAITS = True


def mkvo(cx, v):
    def vo(ins):
        cx.cv.val += 1
        ins.then_inc(cx.cv.h, 1)
    def vw(eng=None):
        if SELF_WAITS or eng is not None:
            (eng or v).wait_ge(cx.cv.h, cx.cv.val)
    return vo, vw


def mkao(cx, s):
    def ao(ins):
        cx.ca.val += 1
        ins.then_inc(cx.ca.h, 1)
    def aw(eng=None):
        if SELF_WAITS or eng is not None:
            (eng or s).wait_ge(cx.ca.h, cx.ca.val)
    return ao, aw


def emit_init(cx):
    nc, I, block = cx.nc, cx.I, cx.block
    sd, cv = cx.sd, cx.cv

    @block.sync
    def _(sync):
        for k, t in cx.cst.items():
            sync.dma_start(out=t[:], in_=I["c_" + k].ap()).then_inc(sd.h, 16)
            sd.val += 16
        for nm, t in cx.sm.items():
            if nm == "gb":
                src = ap(I["gb"], 0, [(0, 128), (1, E)])
            else:
                src = I[nm].ap()
            sync.dma_start(out=t[:], in_=src).then_inc(sd.h, 16)
            sd.val += 16

    @block.vector
    def _(v):
        vo, vw = mkvo(cx, v)
        vw()
        vo(v.memset(cx.cmbT[:], 0.0))
    cx.init_sd = cx.sd.val


def emit_p0(cx):
    nc, block, I = cx.nc, cx.block, cx.I
    sd, sg, cv, ca = cx.sd, cx.sg, cx.cv, cx.ca
    S = cx.S
    NT = 32

    p0 = contextlib.ExitStack()
    with p0:
        S0 = lambda nm, shp, dt=F32: p0.enter_context(nc.sbuf_tensor(nm, shp, dt))
        xt = [S0(f"p0_x{i}", [128, H]) for i in range(3)]
        sq = S0("p0_sq", [128, H])
        xnhi = [S0(f"p0_hi{i}", [128, H], BF16) for i in range(2)]
        xnlo = [S0(f"p0_lo{i}", [128, H], BF16) for i in range(2)]
        xn32 = S0("p0_xn32", [128, H])
        cm = S0("p0_cm", [128, 1]); cq = S0("p0_cq", [128, 1])
        cvv = S0("p0_cv", [128, 1]); rstd = S0("p0_rstd", [128, 1])
        nmr = S0("p0_nmr", [128, 1])

        sdx = [cx.newsem(f"p0x{j}") for j in range(3)]
        sgw = [cx.newsem(f"p0w{j}") for j in range(2)]
        cv0 = cv.val
        ca0 = ca.val
        NV = 9
        NA = 3

        @block.sync
        def _(sync):
            for i in range(NT):
                if i >= 3:
                    sync.wait_ge(cx.ca.h, ca0 + NA * (i - 3) + 1)
                    sync.wait_ge(cx.cv.h, cv0 + NV * (i - 3) + 7)
                sync.dma_start(out=xt[i % 3][:], in_=I["x"].ap()[i * 128:(i + 1) * 128, :]
                               ).then_inc(sdx[i % 3].h, 16)
                sdx[i % 3].val += 16

        @block.vector
        def _(v):
            vo, vw = mkvo(cx, v)
            for i in range(NT):
                xti = xt[i % 3]
                v.wait_ge(sdx[i % 3].h, 16 * (i // 3 + 1))
                vw(); vo(v.reduce_sum(cm[:], xti[:], axis=AX.X))
                v.wait_ge(ca.h, ca0 + NA * i + 1)
                vw(); vo(v.reduce_sum(cq[:], sq[:], axis=AX.X))
                vw(); vo(v.tensor_tensor(cvv[:], cm[:], cm[:], ALU.mult))
                vw(); vo(v.scalar_tensor_tensor(cvv[:], cvv[:], 1.0 / 1024.0, cq[:],
                                                ALU.mult, ALU.subtract))
                vw(); vo(v.tensor_scalar(cvv[:], cvv[:], -1.0 / 1024.0, 1e-5,
                                         ALU.mult, ALU.add))
                v.wait_ge(ca.h, ca0 + NA * i + 3)
                vw(); vo(v.scalar_tensor_tensor(nmr[:], cm[:], -1.0 / 1024.0, rstd[:],
                                                ALU.mult, ALU.mult))
                vw(); vo(v.tensor_scalar(xn32[:], xti[:], rstd[:, 0:1], nmr[:, 0:1],
                                         ALU.mult, ALU.add))
                if i >= 2:
                    v.wait_ge(sgw[i % 2].h, 32 * ((i - 2) // 2 + 1))
                vw(); vo(v.tensor_copy(xnhi[i % 2][:], xn32[:]))
                vw(); vo(v.tensor_tensor(xnlo[i % 2][:], xn32[:], xnhi[i % 2][:],
                                         ALU.subtract))

        @block.scalar
        def _(s):
            ao, aw = mkao(cx, s)
            for i in range(NT):
                s.wait_ge(sdx[i % 3].h, 16 * (i // 3 + 1))
                aw(); ao(s.activation(sq[:], xt[i % 3][:], AF.Square))
                s.wait_ge(cv.h, cv0 + NV * i + 5)
                aw(); ao(s.activation(cvv[:], cvv[:], AF.Ln))
                aw(); ao(s.activation(rstd[:], cvv[:], AF.Exp, scale=-0.5))
        assert cv.val == cv0 + NV * NT
        assert ca.val == ca0 + NA * NT

        @block.gpsimd
        def _(g):
            for i in range(NT):
                g.wait_ge(cv.h, cv0 + NV * i + 8)
                g.dma_start(out=cx.D["dA"].ap()[i * 128:(i + 1) * 128, :],
                            in_=xnhi[i % 2][:]).then_inc(sgw[i % 2].h, 16)
                sgw[i % 2].val += 16
                g.wait_ge(cv.h, cv0 + NV * i + 9)
                g.dma_start(out=cx.D["dA2"].ap()[i * 128:(i + 1) * 128, :],
                            in_=xnlo[i % 2][:]).then_inc(sgw[i % 2].h, 16)
                sgw[i % 2].val += 16
    cx.p0_done = dict(sgw=[x.val for x in sgw], sgw_sems=sgw)


def emit_p0b(cx):
    block, D = cx.block, cx.D
    sa, sg, sd = cx.sa, cx.sg, cx.sd

    sa0 = sa.val
    @block.scalar
    def _(s):
        for sw in cx.p0_done["sgw_sems"]:
            s.wait_ge(sw.h, sw.val)
        for hc in range(8):
            src = ap(D["dA"], hc * 128, [(H, L), (1, 128)])
            s.dma_start(out=cx.xnF[:, hc * L:(hc + 1) * L], in_=src,
                        transpose=True).then_inc(sa.h, 16)
    sa.val += 16 * 8

    @block.gpsimd
    def _(g):
        g.wait_ge(sa.h, sa.val)
        for hc in range(8):
            g.dma_start(out=D["dB"].ap()[hc * 128:(hc + 1) * 128, :],
                        in_=cx.xnF[:, hc * L:(hc + 1) * L]).then_inc(sg.h, 16)
    sg.val += 16 * 8
    cx.p0b_done = dict(sa=sa.val, sg=sg.val)


def emit_p1(cx):
    nc, block, I, D = cx.nc, cx.block, cx.I, cx.D
    sd, sg, sa, cv, ca, ct = cx.sd, cx.sg, cx.sa, cx.cv, cx.ca, cx.ct
    S, PS = cx.S, cx.PS

    p1 = contextlib.ExitStack()
    with p1:
        S1 = lambda nm, shp, dt=F32: p1.enter_context(nc.sbuf_tensor(nm, shp, dt))
        gwf = S1("p1_gwf", [128, 128])
        gwhi = S1("p1_gwhi", [128, 128], BF16)
        gwlo = S1("p1_gwlo", [128, 128], BF16)
        xlo = [S1(f"p1_xlo{i}", [128, L], BF16) for i in range(2)]
        sc = S1("p1_sc", [128, 512])
        exn = S1("p1_ex", [128, 512])
        m32 = S1("p1_m32", [128, 32]); s32 = S1("p1_s32", [128, 32]); r32 = S1("p1_r32", [128, 32])
        orig = S1("p1_orig", [128, 512]); bsd = S1("p1_bsd", [128, 512])
        gsum = [S1(f"p1_gs{i}", [128, 128]) for i in range(2)]
        gsc = S1("p1_gsc", [128, 128])
        g32a = S1("p1_g32a", [128, 32]); g32b = S1("p1_g32b", [128, 32])
        g32c = S1("p1_g32c", [128, 32]); sec = S1("p1_sec", [128, 32])
        keep = S1("p1_keep", [128, 128])
        msk = S1("p1_msk", [128, 512]); msk2 = S1("p1_msk2", [128, 512])
        sel = S1("p1_sel", [128, 512])
        cmb = S1("p1_cmb", [128, 4096], BF16)

        sd0 = sd.val
        @block.sync
        def _(sync):
            quiesce(cx, sync)
            sync.dma_start(out=gwf[:],
                           in_=ap(I["gw_t"], 0, [(16, 128), (128 * 16, 8), (1, 16)])
                           ).then_inc(sd.h, 16)
        sd.val += 16

        @block.vector
        def _(v):
            vo, vw = mkvo(cx, v)
            quiesce(cx, v)
            v.wait_ge(sd.h, sd0 + 16)
            vw(); vo(v.tensor_copy(gwhi[:], gwf[:]))
            vw(); vo(v.tensor_tensor(gwlo[:], gwf[:], gwhi[:], ALU.subtract))
        cv_gw = cx.cv.val

        sxl = [cx.newsem(f"p1xl{j}") for j in range(2)]
        ct0 = ct.val
        @block.scalar
        def _(s):
            quiesce(cx, s)
            for dc in range(8):
                if dc >= 2:
                    s.wait_ge(ct.h, ct0 + dc - 1)
                src = ap(D["dA2"], dc * 128, [(H, L), (1, 128)])
                s.dma_start(out=xlo[dc % 2][:], in_=src, transpose=True).then_inc(sxl[dc % 2].h, 16)
                sxl[dc % 2].val += 16

        pband = PS[:, 0:512]
        @block.tensor
        def _(t):
            t.wait_ge(cx.cv.h, cv_gw)
            t.wait_ge(cx.sa.h, cx.p0b_done["sa"])
            first = True
            for dc in range(8):
                t.wait_ge(sxl[dc % 2].h, 16 * (dc // 2 + 1))
                for tc in range(32):
                    lhs_hi = cx.xnF[:, dc * L + tc * 128: dc * L + tc * 128 + 128]
                    lhs_lo = xlo[dc % 2][:, tc * 128:tc * 128 + 128]
                    rhs_hi = gwhi[:, dc * 16:dc * 16 + 16]
                    rhs_lo = gwlo[:, dc * 16:dc * 16 + 16]
                    og = pband[:, tc * 16:tc * 16 + 16]
                    last = (dc == 7) and (tc == 31)
                    t.matmul(og, lhs_hi, rhs_hi, start=first, stop=False)
                    first = False
                    t.matmul(og, lhs_hi, rhs_lo, start=False, stop=False)
                    ins = t.matmul(og, lhs_lo, rhs_hi, start=False, stop=last)
                    if tc == 31:
                        ins.then_inc(ct.h, 1)
        ct.val += 8

        @block.vector
        def _(v):
            vo, vw = mkvo(cx, v)
            vw(); vo(v.memset(cmb[:], 0.0))
            v.wait_ge(ct.h, ct.val)
            src3 = ap(cx.PS, 0, [(4096, 128), (16, 32), (1, 16)])
            vw(); vo(v.tensor_reduce(m32[:], src3, AX.X, ALU.max))
            mb = ap(m32, 0, [(32, 128), (1, 32), (0, 16)])
            vw(); vo(v.tensor_tensor(sc[:], pband, mb, ALU.subtract))
        @block.scalar
        def _(s):
            ao, aw = mkao(cx, s)
            s.wait_ge(cx.cv.h, cx.cv.val)
            aw(); ao(s.activation(exn[:], sc[:], AF.Exp))
        @block.vector
        def _(v):
            vo, vw = mkvo(cx, v)
            v.wait_ge(cx.ca.h, cx.ca.val)
            e3 = ap(exn, 0, [(512, 128), (16, 32), (1, 16)])
            vw(); vo(v.tensor_reduce(s32[:], e3, AX.X, ALU.add))
            vw(); vo(v.reciprocal(r32[:], s32[:]))
            rb = ap(r32, 0, [(32, 128), (1, 32), (0, 16)])
            vw(); vo(v.tensor_tensor(orig[:], exn[:], rb, ALU.mult))
            gbb = ap(cx.sm["gb"], 0, [(16, 128), (0, 32), (1, 16)])
            vw(); vo(v.tensor_tensor(bsd[:], orig[:], gbb, ALU.add))
            def gview(t, j):
                return ap(t, j, [(512, 128), (16, 32), (4, 4)])
            pair = 0
            firstp = True
            for i in range(4):
                for j in range(i + 1, 4):
                    dst = gsum[pair % 2]
                    vw(); vo(v.tensor_tensor(dst[:], gview(bsd, i), gview(bsd, j), ALU.add))
                    if firstp:
                        vw(); vo(v.tensor_copy(gsc[:], dst[:]))
                        firstp = False
                    else:
                        vw(); vo(v.tensor_tensor(gsc[:], gsc[:], dst[:], ALU.max))
                    pair += 1
            def cview(t, g):
                return ap(t, g, [(128, 128), (4, 32)])
            vw(); vo(v.tensor_tensor(g32a[:, 0:32], cview(gsc, 0), cview(gsc, 1), ALU.max))
            vw(); vo(v.tensor_tensor(g32b[:, 0:32], cview(gsc, 0), cview(gsc, 1), ALU.min))
            vw(); vo(v.tensor_tensor(g32c[:, 0:32], cview(gsc, 2), cview(gsc, 3), ALU.max))
            vw(); vo(v.tensor_tensor(sec[:, 0:32], cview(gsc, 2), cview(gsc, 3), ALU.min))
            vw(); vo(v.tensor_tensor(g32a[:, 0:32], g32a[:, 0:32], g32c[:, 0:32], ALU.min))
            vw(); vo(v.tensor_tensor(g32b[:, 0:32], g32b[:, 0:32], sec[:, 0:32], ALU.max))
            vw(); vo(v.tensor_tensor(sec[:, 0:32], g32a[:, 0:32], g32b[:, 0:32], ALU.max))
            secb = ap(sec, 0, [(32, 128), (1, 32), (0, 4)])
            vw(); vo(v.tensor_tensor(keep[:], gsc[:], secb, ALU.is_ge))
            kb = ap(keep, 0, [(128, 128), (4, 32), (1, 4), (0, 4)])
            vw(); vo(v.tensor_scalar(msk[:], kb, 1e9, -1e9, ALU.mult, ALU.add))
            vw(); vo(v.tensor_tensor(msk[:], msk[:], bsd[:], ALU.add))
            m3 = ap(msk, 0, [(512, 128), (16, 32), (1, 16)])
            vw(); vo(v.tensor_reduce(m32[:], m3, AX.X, ALU.max))
            vw(); vo(v.tensor_copy(g32c[:, 0:32], m32[:]))
            mb1 = ap(m32, 0, [(32, 128), (1, 32), (0, 16)])
            vw(); vo(v.tensor_tensor(sel[:], msk[:], mb1, ALU.is_ge))
            vw(); vo(v.tensor_copy(gsc[:], sel[:, 0:128]))
            vw(); vo(v.scalar_tensor_tensor(msk2[:], sel[:], -1e9, msk[:], ALU.mult, ALU.add))
            m23 = ap(msk2, 0, [(512, 128), (16, 32), (1, 16)])
            vw(); vo(v.tensor_reduce(s32[:], m23, AX.X, ALU.max))
            sb2 = ap(s32, 0, [(32, 128), (1, 32), (0, 16)])
            vw(); vo(v.tensor_tensor(sel[:], msk[:], sb2, ALU.is_ge))
            cdst = ap(cmb, 0, [(4096, 128), (128, 32), (1, 16)])
            vw(); vo(v.tensor_tensor(cdst, sel[:], orig[:], ALU.mult))
        cx.p1_cmb_cv = cx.cv.val

        if "dbg0" in cx.taps:
            @block.gpsimd
            def _(g):
                g.wait_ge(cx.cv.h, cx.p1_cmb_cv)
                for j, t in enumerate([sc, exn, orig, bsd, msk, sel, None, None, msk2, m32, s32]):
                    if t is None:
                        continue
                    g.dma_start(out=cx.taps[f"dbg{j}"].ap(), in_=t[:]).then_inc(sg.h, 16)
                    sg.val += 16
                g.dma_start(out=cx.taps["dbg6"].ap(), in_=gsc[:]).then_inc(sg.h, 16)
                g.dma_start(out=cx.taps["dbg7"].ap(), in_=g32c[:, 0:32]).then_inc(sg.h, 16)
                sg.val += 32

        sg0 = sg.val
        @block.gpsimd
        def _(g):
            g.wait_ge(cx.cv.h, cx.p1_cmb_cv)
            dst = ap(D["dCMB"], 0, [(128, 128), (128 * 128, 32), (1, 128)])
            g.dma_start(out=dst, in_=cmb[:]).then_inc(sg.h, 16)
        sg.val += 16

        @block.scalar
        def _(s):
            s.wait_ge(sg.h, sg.val)
            s.wait_ge(cx.cv.h, 1)
            s.dma_start(out=cx.cmbT[0:128, :], in_=D["dCMB"].ap(), transpose=True
                        ).then_inc(sa.h, 16)
        sa.val += 16

        @block.gpsimd
        def _(g):
            g.wait_ge(sa.h, sa.val)
            g.dma_start(out=D["dCMBT"].ap(), in_=cx.cmbT[0:16, :]).then_inc(sg.h, 16)
        sg.val += 16
    cx.p1_done = dict(sg=sg.val, sa=sa.val, cv=cx.cv.val)


def fftf_pe(cx, t, m2buf, m2off, GB, XB, dve_ready, grbuf, gibuf, xb_free_ca=None):
    ct, cst = cx.ct, cx.cst
    for pr in range(4):
        bank = GB[pr // 2]
        og = bank[:, (pr % 2) * 256:(pr % 2) * 256 + 256]
        ins = t.matmul(og, m2buf[:, m2off + pr * 128: m2off + pr * 128 + 128],
                       cst["cf"][:], start=(pr % 2 == 0), stop=(pr % 2 == 1))
        if pr == 3:
            ins.then_inc(ct.h, 1)
    ct.val += 1
    t.wait_ge(cx.cv.h, dve_ready)
    if xb_free_ca is not None:
        t.wait_ge(cx.ca.h, xb_free_ca)
    seq = [("s2wr", grbuf, 0, "start"), ("s2win", gibuf, 0, "stop"),
           ("s2wi", grbuf, 1, "start"), ("s2wr", gibuf, 1, "stop")]
    ins = None
    for cname, src, xbank, role in seq:
        for sl in range(2):
            ins = t.matmul(XB[xbank][:, sl * 256:sl * 256 + 256],
                           cst[cname][:], src[:, sl * 256:sl * 256 + 256],
                           start=(role == "start" and sl == 0),
                           stop=(role == "stop" and sl == 1))
    ins.then_inc(ct.h, 1)
    ct.val += 1


def fftf_act(cx, s, GB, XB, gbuf, xbuf, pe_base, ao, aw):
    s.wait_ge(cx.ct.h, pe_base + 1)
    aw(); ao(s.activation(gbuf[:, 0:512], GB[0][:], AF.Copy))
    aw(); ao(s.activation(gbuf[:, 512:1024], GB[1][:], AF.Copy))
    s.wait_ge(cx.ct.h, pe_base + 2)
    aw(); ao(s.activation(xbuf[:, 0:512], XB[0][:], AF.Copy))
    aw(); ao(s.activation(xbuf[:, 512:1024], XB[1][:], AF.Copy))


def fftf_dve(cx, v, gbuf, grbuf, gibuf, tw1, tw2, act_base, vo, vw):
    grv = ap(gbuf, 0, [(1024, 128), (256, 4), (1, 128)])
    giv = ap(gbuf, 128, [(1024, 128), (256, 4), (1, 128)])
    t1rb = ap(cx.cst["t1r"], 0, [(128, 128), (0, 4), (1, 128)])
    t1ib = ap(cx.cst["t1i"], 0, [(128, 128), (0, 4), (1, 128)])
    v.wait_ge(cx.ca.h, act_base + 2)
    vw(); vo(v.tensor_tensor(tw1[:], grv, t1rb, ALU.mult))
    vw(); vo(v.tensor_tensor(tw2[:], giv, t1ib, ALU.mult))
    vw(); vo(v.tensor_tensor(grbuf[:], tw1[:], tw2[:], ALU.subtract))
    vw(); vo(v.tensor_tensor(tw1[:], grv, t1ib, ALU.mult))
    vw(); vo(v.tensor_tensor(tw2[:], giv, t1rb, ALU.mult))
    vw(); vo(v.tensor_tensor(gibuf[:], tw1[:], tw2[:], ALU.add))


def clamppi(v, vo, vw, buf):
    vw(); vo(v.tensor_scalar(buf[:], buf[:], math.pi, None, ALU.min))
    vw(); vo(v.tensor_scalar(buf[:], buf[:], -math.pi, None, ALU.max))


def emit_p2(cx):
    nc, block, I, D = cx.nc, cx.block, cx.I, cx.D
    sd, sg, cv, ca, ct = cx.sd, cx.sg, cx.cv, cx.ca, cx.ct
    PS = cx.PS
    HH = H

    st5 = p5_setup(cx)
    p2 = contextlib.ExitStack()
    with p2:
        S2_ = lambda nm, shp, dt=F32: p2.enter_context(nc.sbuf_tensor(nm, shp, dt))
        ucr = S2_("p2_ucr", [64, H]); uci = S2_("p2_uci", [64, H])
        BM = [{nm: S2_(f"p2c{j}_" + nm, [64, 256], BF16) for nm in
               ["pr", "pi", "wr", "wi", "qr", "qi", "ar", "ai"]} for j in range(2)]
        Ur = S2_("p2_Ur", [128, 2048], BF16)
        Ui = S2_("p2_Ui", [128, 2048], BF16)
        Vr = S2_("p2_Vr", [128, 2048], BF16)
        Vi = S2_("p2_Vi", [128, 2048], BF16)
        km2 = S2_("p2_km2", [128, 2048], BF16)
        gbuf = [S2_(f"p2_gbuf{j}", [128, 1024], BF16) for j in range(2)]
        grbuf = [S2_(f"p2_grb{j}", [128, 512], BF16) for j in range(2)]
        gibuf = [S2_(f"p2_gib{j}", [128, 512], BF16) for j in range(2)]
        xbuf = S2_("p2_xbuf", [128, 1024], BF16)
        tw1 = S2_("p2_tw1", [128, 512], BF16)
        tw2 = S2_("p2_tw2", [128, 512], BF16)

        dWP = D["dWP"]
        swk = cx.newsem("p2wk")
        sgk = cx.newsem("p2k")
        swp = cx.newsem("p2wp")
        sdt = cx.newsem("p2dt")
        cx.p2_sgk = sgk

        with contextlib.ExitStack() as pp:
            SP = lambda nm, shp, dt=F32: pp.enter_context(nc.sbuf_tensor(nm, shp, dt))
            T = {nm: SP("p2_" + nm, [64, HH]) for nm in
                 ["dtb", "rho", "th", "r0", "r1", "r2", "r3", "r4", "r5", "r6"]}
            ldt = SP("p2_ldt", [1, H])

            sd0 = sd.val
            @block.scalar
            def _(s):
                ao, aw = mkao(cx, s)
                quiesce(cx, s)
            @block.gpsimd
            def _(g):
                quiesce(cx, g)
            @block.vector
            def _(v):
                vo, vw = mkvo(cx, v)
                quiesce(cx, v)
            sld = cx.newsem("p2ld")
            skp = cx.newsem("p2in")
            @block.sync
            def _(sync):
                quiesce(cx, sync)
                sync.dma_start(out=ldt[:], in_=I["logdt_t"].ap()).then_inc(sld.h, 16)
                sld.val += 16
            @block.scalar
            def _(s):
                ao, aw = mkao(cx, s)
                s.wait_ge(sld.h, 16)
                aw(); ao(s.activation(ldt[:], ldt[:], AF.Exp))
            @block.gpsimd
            def _(g):
                g.wait_ge(ca.h, ca.val)
                g.dma_start(out=D["dDT"].ap(), in_=ldt[0:1, :]).then_inc(sdt.h, 16)
                sdt.val += 16

            def redcos(v, vo, vw, src, ang, msk):
                vw(); vo(v.tensor_scalar(ang[:], src[:], 1.0 / TWO_PI, MAGIC, ALU.mult, ALU.add))
                vw(); vo(v.tensor_scalar(ang[:], ang[:], MAGIC, None, ALU.subtract))
                vw(); vo(v.scalar_tensor_tensor(msk[:], ang[:], -RC1, src[:], ALU.mult, ALU.add))
                vw(); vo(v.scalar_tensor_tensor(ang[:], ang[:], -RC2, msk[:], ALU.mult, ALU.add))
                clamppi(v, vo, vw, ang)
                vw(); vo(v.tensor_scalar(msk[:], ang[:], math.pi / 2.0, None, ALU.is_gt))
                vw(); vo(v.scalar_tensor_tensor(msk[:], msk[:], -TWO_PI, ang[:], ALU.mult, ALU.add))
                vw(); vo(v.tensor_scalar(msk[:], msk[:], math.pi / 2.0, None, ALU.add))
                clamppi(v, vo, vw, msk)

            for hf in range(1):
                h0 = hf * HH
                @block.sync
                def _(sync):
                    if hf > 0:
                        sync.wait_ge(swp.h, swp.val)
                    for nm, inm in [("r1", "arl_t"), ("r2", "aim_t"), ("r3", "cr_t"),
                                    ("r4", "ci_t")]:
                        sync.dma_start(out=T[nm][:],
                                       in_=I[inm].ap()[:, h0:h0 + HH]).then_inc(skp.h, 16)
                        skp.val += 16
                @block.gpsimd
                def _(g):
                    g.wait_ge(sdt.h, 16)
                    if hf > 0:
                        g.wait_ge(swp.h, swp.val)
                    g.dma_start(out=T["dtb"][:],
                                in_=ap(D["dDT"], h0, [(0, 64), (1, HH)])).then_inc(sdt.h, 16)
                    sdt.val += 16
                @block.scalar
                def _(s):
                    ao, aw = mkao(cx, s)
                    s.wait_ge(skp.h, skp.val)
                    if hf > 0:
                        s.wait_ge(cv.h, cv.val)
                    aw(); ao(s.activation(T["r0"][:], T["r1"][:], AF.Exp))
                @block.vector
                def _(v):
                    vo, vw = mkvo(cx, v)
                    m = T
                    v.wait_ge(sdt.h, sdt.val)
                    v.wait_ge(ca.h, ca.val)
                    vw(); vo(v.scalar_tensor_tensor(m["rho"][:], m["r0"][:], -1.0, m["dtb"][:], ALU.mult, ALU.mult))
                    vw(); vo(v.tensor_tensor(m["th"][:], m["r2"][:], m["dtb"][:], ALU.mult))
                    redcos(v, vo, vw, m["th"], m["r5"], m["r6"])
                @block.scalar
                def _(s):
                    ao, aw = mkao(cx, s)
                    s.wait_ge(cv.h, cv.val)
                    aw(); ao(s.activation(T["r5"][:], T["r5"][:], AF.Sin))
                    aw(); ao(s.activation(T["r6"][:], T["r6"][:], AF.Sin))
                    aw(); ao(s.activation(T["r1"][:], T["rho"][:], AF.Exp))
                @block.vector
                def _(v):
                    vo, vw = mkvo(cx, v)
                    m = T
                    ucr_h = ucr[:, h0:h0 + HH]
                    uci_h = uci[:, h0:h0 + HH]
                    v.wait_ge(ca.h, ca.val)
                    vw(); vo(v.tensor_tensor(m["r6"][:], m["r1"][:], m["r6"][:], ALU.mult))
                    vw(); vo(v.tensor_scalar(m["r6"][:], m["r6"][:], 1.0, None, ALU.subtract))
                    vw(); vo(v.tensor_tensor(m["r5"][:], m["r1"][:], m["r5"][:], ALU.mult))
                    vw(); vo(v.reciprocal(m["r1"][:], m["dtb"][:]))
                    vw(); vo(v.tensor_tensor(m["r1"][:], m["th"][:], m["r1"][:], ALU.mult))
                    vw(); vo(v.tensor_tensor(m["r2"][:], m["r0"][:], m["r0"][:], ALU.mult))
                    vw(); vo(v.tensor_tensor(ucr_h, m["r1"][:], m["r1"][:], ALU.mult))
                    vw(); vo(v.tensor_tensor(m["r2"][:], m["r2"][:], ucr_h, ALU.add))
                    vw(); vo(v.reciprocal(m["r2"][:], m["r2"][:]))
                    vw(); vo(v.tensor_tensor(ucr_h, m["r3"][:], m["r6"][:], ALU.mult))
                    vw(); vo(v.tensor_tensor(uci_h, m["r4"][:], m["r5"][:], ALU.mult))
                    vw(); vo(v.tensor_tensor(ucr_h, ucr_h, uci_h, ALU.subtract))
                    vw(); vo(v.tensor_tensor(uci_h, m["r3"][:], m["r5"][:], ALU.mult))
                    vw(); vo(v.tensor_tensor(m["r3"][:], m["r4"][:], m["r6"][:], ALU.mult))
                    vw(); vo(v.tensor_tensor(uci_h, uci_h, m["r3"][:], ALU.add))
                    vw(); vo(v.tensor_tensor(m["r3"][:], uci_h, m["r1"][:], ALU.mult))
                    vw(); vo(v.tensor_tensor(m["r4"][:], ucr_h, m["r0"][:], ALU.mult))
                    vw(); vo(v.tensor_tensor(m["r3"][:], m["r3"][:], m["r4"][:], ALU.subtract))
                    vw(); vo(v.tensor_tensor(m["r4"][:], ucr_h, m["r1"][:], ALU.mult))
                    vw(); vo(v.tensor_tensor(m["r5"][:], uci_h, m["r0"][:], ALU.mult))
                    vw(); vo(v.tensor_tensor(m["r4"][:], m["r4"][:], m["r5"][:], ALU.add))
                    vw(); vo(v.tensor_tensor(ucr_h, m["r3"][:], m["r2"][:], ALU.mult))
                    vw(); vo(v.tensor_scalar(ucr_h, ucr_h, 2.0, None, ALU.mult))
                    vw(); vo(v.tensor_tensor(uci_h, m["r4"][:], m["r2"][:], ALU.mult))
                    vw(); vo(v.tensor_scalar(uci_h, uci_h, 2.0, None, ALU.mult))
                for k in range(12):
                    @block.vector
                    def _(v):
                        vo, vw = mkvo(cx, v)
                        m = T
                        v.wait_ge(ca.h, ca.val)
                        if k > 0 or hf > 0:
                            v.wait_ge(swp.h, swp.val)
                        sc = float(2 ** k)
                        vw(); vo(v.tensor_scalar(m["r1"][:], m["th"][:], sc, None, ALU.mult))
                        redcos(v, vo, vw, m["r1"], m["r3"], m["r4"])
                        vw(); vo(v.tensor_scalar(m["r5"][:], m["rho"][:], sc, None, ALU.mult))
                    @block.scalar
                    def _(s):
                        ao, aw = mkao(cx, s)
                        s.wait_ge(cv.h, cv.val)
                        aw(); ao(s.activation(T["r3"][:], T["r3"][:], AF.Sin))
                        aw(); ao(s.activation(T["r4"][:], T["r4"][:], AF.Sin))
                        aw(); ao(s.activation(T["r5"][:], T["r5"][:], AF.Exp))
                    @block.vector
                    def _(v):
                        vo, vw = mkvo(cx, v)
                        m = T
                        v.wait_ge(ca.h, ca.val)
                        vw(); vo(v.tensor_tensor(m["r1"][:], m["r5"][:], m["r4"][:], ALU.mult))
                        vw(); vo(v.scalar_tensor_tensor(m["r2"][:], m["r5"][:], -1.0, m["r3"][:], ALU.mult, ALU.mult))
                    @block.gpsimd
                    def _(g):
                        g.wait_ge(cv.h, cv.val)
                        g.dma_start(out=dWP.ap()[(2 * k) * 64:(2 * k) * 64 + 64, h0:h0 + HH],
                                    in_=T["r1"][:]).then_inc(swp.h, 16)
                        g.dma_start(out=dWP.ap()[(2 * k + 1) * 64:(2 * k + 1) * 64 + 64, h0:h0 + HH],
                                    in_=T["r2"][:]).then_inc(swp.h, 16)
                        swp.val += 32

        tA = S2_("p2_tA", [64, 2048], BF16); tB = S2_("p2_tB", [64, 2048], BF16)
        tC = S2_("p2_tC", [64, 2048], BF16); tD = S2_("p2_tD", [64, 2048], BF16)

        dF8 = D["dF8"]
        sfb = cx.newsem("p2fb")
        sld2 = cx.newsem("p2ml")
        with contextlib.ExitStack() as pb:
            SB = lambda nm, shp, dt=F32: pb.enter_context(nc.sbuf_tensor(nm, shp, dt))
            tbr = SB("p2_tbr", [64, 2048]); tbi = SB("p2_tbi", [64, 2048])
            mr3 = SB("p2_mr3", [64, 768]); mi3 = SB("p2_mi3", [64, 768])
            tp1b = SB("p2_t1b", [64, 1024]); tp2b = SB("p2_t2b", [64, 1024])
            tabs = [(0, True, 0), (3, True, 2), (6, False, 4), (9, False, 6)]
            nfb = 0
            for c in range(4):
                for (kbase, cj, pl) in tabs:
                    @block.sync
                    def _(sync, c=c, kbase=kbase, nfb=nfb):
                        if nfb == 0:
                            sync.wait_ge(swp.h, swp.val)
                        else:
                            sync.wait_ge(cv.h, cv.val)
                        for st_ in range(3):
                            k = kbase + st_
                            sync.dma_start(out=mr3[:, st_ * 256:st_ * 256 + 256],
                                           in_=dWP.ap()[(2 * k) * 64:(2 * k) * 64 + 64,
                                                        c * 256:c * 256 + 256]).then_inc(sld2.h, 16)
                            sync.dma_start(out=mi3[:, st_ * 256:st_ * 256 + 256],
                                           in_=dWP.ap()[(2 * k + 1) * 64:(2 * k + 1) * 64 + 64,
                                                        c * 256:c * 256 + 256]).then_inc(sld2.h, 16)
                            sld2.val += 32
                    @block.vector
                    def _(v, cj=cj, nfb=nfb):
                        vo, vw = mkvo(cx, v)
                        v.wait_ge(sld2.h, sld2.val)
                        if nfb >= 1:
                            v.wait_ge(sfb.h, 32 * nfb)
                        vw(); vo(v.memset(ap(tbr, 0, [(2048, 64), (8, 256), (1, 1)]), 1.0))
                        vw(); vo(v.memset(ap(tbi, 0, [(2048, 64), (8, 256), (1, 1)]), 0.0))
                        for st_ in range(3):
                            Tn = 2 ** st_
                            sr = ap(tbr, 0, [(2048, 64), (8, 256), (1, Tn)])
                            si = ap(tbi, 0, [(2048, 64), (8, 256), (1, Tn)])
                            dr = ap(tbr, Tn, [(2048, 64), (8, 256), (1, Tn)])
                            di = ap(tbi, Tn, [(2048, 64), (8, 256), (1, Tn)])
                            wr_ = ap(mr3, st_ * 256, [(768, 64), (1, 256), (0, Tn)])
                            wi_ = ap(mi3, st_ * 256, [(768, 64), (1, 256), (0, Tn)])
                            t1v = ap(tp1b, 0, [(1024, 64), (Tn, 256), (1, Tn)])
                            t2v = ap(tp2b, 0, [(1024, 64), (Tn, 256), (1, Tn)])
                            vw(); vo(v.tensor_tensor(t1v, sr, wr_, ALU.mult))
                            vw(); vo(v.tensor_tensor(t2v, si, wi_, ALU.mult))
                            vw(); vo(v.tensor_tensor(dr, t1v, t2v,
                                                     ALU.subtract if cj else ALU.add))
                            vw(); vo(v.tensor_tensor(t1v, sr, wi_, ALU.mult))
                            vw(); vo(v.tensor_tensor(t2v, si, wr_, ALU.mult))
                            if cj:
                                vw(); vo(v.tensor_tensor(di, t1v, t2v, ALU.add))
                            else:
                                vw(); vo(v.tensor_tensor(di, t2v, t1v, ALU.subtract))
                    @block.gpsimd
                    def _(g, c=c, pl=pl):
                        g.wait_ge(cv.h, cv.val)
                        g.dma_start(out=ap(dF8, pl * 8192 + c * 2048, [(65536, 64), (1, 2048)]),
                                    in_=tbr[:]).then_inc(sfb.h, 16)
                        g.dma_start(out=ap(dF8, (pl + 1) * 8192 + c * 2048, [(65536, 64), (1, 2048)]),
                                    in_=tbi[:]).then_inc(sfb.h, 16)
                        sfb.val += 32
                    nfb += 1

        for c5 in range(8):
            p5_chunk(cx, st5, c5)

        NKG = 32
        DPEg = 4 + 8
        DAg = 4 + 16
        DVg = 18 + 24
        ct0 = ct.val
        ca0 = ca.val

        @block.vector
        def _(v):
            vo, vw = mkvo(cx, v)
            for bb in [Ur, Ui, Vr, Vi, km2]:
                vw(); vo(v.memset(bb[:], 0.0))
        cv0 = cv.val

        FPL = ["pr", "pi", "wr", "wi", "qr", "qi", "ar", "ai"]

        @block.sync
        def _(sync):
            sync.wait_ge(sfb.h, sfb.val)
            for g_ in range(NKG):
                if g_ >= 1:
                    sync.wait_ge(swk.h, swk.val)
                if g_ >= 2:
                    sync.wait_ge(cv.h, cv0 + (g_ - 2) * DVg + 18)
                for pl, nm in enumerate(FPL):
                    src = ap(dF8, pl * 8192 + g_ * 256, [(65536, 64), (1, 256)])
                    sync.dma_start(out=BM[g_ % 2][nm][:], in_=src).then_inc(swk.h, 16)
                    swk.val += 16

        GB = (PS[:, 0:512], PS[:, 512:1024])
        XB = (PS[:, 1024:1536], PS[:, 1536:2048])

        for g_ in range(NKG):
            v0 = cv0 + g_ * DVg
            a0g = ca0 + g_ * DAg
            pe0g = ct0 + g_ * DPEg
            @block.vector
            def _(v):
                vo, vw = mkvo(cx, v)
                v.wait_ge(swk.h, 128 * (g_ + 1))
                if g_ > 0:
                    v.wait_ge(ct.h, pe0g - DPEg + 4)
                bview = lambda t_: ap(BM[g_ % 2][t_], 0, [(256, 64), (8, 32), (0, 8), (1, 8)])
                aview = lambda t_: ap(BM[g_ % 2][t_], 0, [(256, 64), (8, 32), (1, 8), (0, 8)])
                sview = lambda t_: ap(t_, g_ * 32, [(H, 64), (1, 32), (0, 8), (0, 8)])
                tAv = ap(tA, 0, [(2048, 64), (1, 2048)])
                tBv = ap(tB, 0, [(2048, 64), (1, 2048)])
                tCv = ap(tC, 0, [(2048, 64), (1, 2048)])
                tDv = ap(tD, 0, [(2048, 64), (1, 2048)])
                vw(); vo(v.tensor_tensor(tAv, bview("pr"), aview("wr"), ALU.mult))
                vw(); vo(v.tensor_tensor(tBv, bview("pi"), aview("wi"), ALU.mult))
                vw(); vo(v.tensor_tensor(tAv, tAv, tBv, ALU.subtract))
                vw(); vo(v.tensor_tensor(tBv, bview("pr"), aview("wi"), ALU.mult))
                vw(); vo(v.tensor_tensor(tCv, bview("pi"), aview("wr"), ALU.mult))
                vw(); vo(v.tensor_tensor(tBv, tBv, tCv, ALU.add))
                vw(); vo(v.tensor_tensor(tCv, sview(ucr), tAv, ALU.mult))
                vw(); vo(v.tensor_tensor(tDv, sview(uci), tBv, ALU.mult))
                vw(); vo(v.tensor_tensor(Ur[0:64, :], tCv, tDv, ALU.subtract))
                vw(); vo(v.tensor_tensor(tCv, sview(ucr), tBv, ALU.mult))
                vw(); vo(v.tensor_tensor(tDv, sview(uci), tAv, ALU.mult))
                vw(); vo(v.tensor_tensor(Ui[0:64, :], tCv, tDv, ALU.add))
                vw(); vo(v.tensor_tensor(tAv, bview("qr"), aview("ar"), ALU.mult))
                vw(); vo(v.tensor_tensor(tBv, bview("qi"), aview("ai"), ALU.mult))
                vw(); vo(v.tensor_tensor(Vr[0:64, :], tAv, tBv, ALU.subtract))
                vw(); vo(v.tensor_tensor(tAv, bview("qr"), aview("ai"), ALU.mult))
                vw(); vo(v.tensor_tensor(tBv, bview("qi"), aview("ar"), ALU.mult))
                vw(); vo(v.tensor_tensor(Vi[0:64, :], tAv, tBv, ALU.add))
            @block.tensor
            def _(t):
                if g_ == 0:
                    t.wait_ge(cv.h, v0 + 18)
                else:
                    t.wait_ge(cv.h, v0 + 18)
                    t.wait_ge(ca.h, a0g)
                for hl in range(32):
                    og = bass.AP(cx.PS, ((hl // 8) % 2) * 512 + (hl % 8) * 64,
                                 [[4096, 64], [1, 64]])
                    if hl % 8 == 0 and hl >= 16:
                        t.wait_ge(ca.h, a0g + hl // 8 - 1)
                    t.matmul(og, Vr[:, hl * 64:hl * 64 + 64],
                             Ur[:, hl * 64:hl * 64 + 64],
                             start=(hl % 8 == 0), stop=False)
                    ins = t.matmul(og, Vi[:, hl * 64:hl * 64 + 64],
                                   Ui[:, hl * 64:hl * 64 + 64],
                                   start=False, stop=(hl % 8 == 7))
                    if hl % 8 == 7:
                        ins.then_inc(ct.h, 1)
                        ct.val += 1
            @block.scalar
            def _(s):
                ao, aw = mkao(cx, s)
                if g_ > 0:
                    s.wait_ge(ct.h, pe0g)
                for gq in range(4):
                    s.wait_ge(ct.h, pe0g + gq + 1)
                    pb = bass.AP(cx.PS, (gq % 2) * 512, [[4096, 64], [1, 512]])
                    dst = bass.AP(km2, gq * 512, [[2048, 64], [1, 512]])
                    aw(); ao(s.activation(dst, pb, AF.Copy))
            K_ = pe0g + 4
            s1_ct = {0: K_ + 1, 1: K_ + 2, 2: K_ + 4, 3: K_ + 6}
            s2_ct = {0: K_ + 3, 1: K_ + 5, 2: K_ + 7, 3: K_ + 8}
            gev_end = {0: a0g + 6, 1: a0g + 8, 2: a0g + 12, 3: a0g + 16}
            xev_end = {0: a0g + 10, 1: a0g + 14, 2: a0g + 18, 3: a0g + 20}
            tw_cv = {b: v0 + 18 + 6 * (b + 1) for b in range(4)}
            sgk0 = sgk.val
            for it in range(5):
                @block.tensor
                def _(t, it=it):
                    if it <= 3:
                        t.wait_ge(ca.h, a0g + 4)
                        if it >= 1:
                            t.wait_ge(ca.h, gev_end[it - 1])
                        ins = None
                        for pr in range(4):
                            og = GB[pr // 2][:, (pr % 2) * 256:(pr % 2) * 256 + 256]
                            ins = t.matmul(og, km2[:, it * 512 + pr * 128: it * 512 + pr * 128 + 128],
                                           cx.cst["cf"][:], start=(pr % 2 == 0), stop=(pr % 2 == 1))
                        ins.then_inc(ct.h, 1); ct.val += 1
                    if it >= 1:
                        b = it - 1
                        t.wait_ge(cv.h, tw_cv[b])
                        if b >= 1:
                            t.wait_ge(ca.h, xev_end[b - 1])
                        elif g_ > 0:
                            t.wait_ge(ca.h, a0g)
                        seq = [("s2wr", grbuf[b % 2], 0, "start"), ("s2win", gibuf[b % 2], 0, "stop"),
                               ("s2wi", grbuf[b % 2], 1, "start"), ("s2wr", gibuf[b % 2], 1, "stop")]
                        ins = None
                        for cname, srcb, xbank, role in seq:
                            for sl in range(2):
                                ins = t.matmul(XB[xbank][:, sl * 256:sl * 256 + 256],
                                               cx.cst[cname][:], srcb[:, sl * 256:sl * 256 + 256],
                                               start=(role == "start" and sl == 0),
                                               stop=(role == "stop" and sl == 1))
                        ins.then_inc(ct.h, 1); ct.val += 1
                @block.scalar
                def _(s, it=it):
                    ao, aw = mkao(cx, s)
                    if it <= 3:
                        s.wait_ge(ct.h, s1_ct[it])
                        if it >= 2:
                            s.wait_ge(cv.h, tw_cv[it - 2])
                        elif g_ > 0:
                            s.wait_ge(cv.h, v0)
                        aw(); ao(s.activation(gbuf[it % 2][:, 0:512], GB[0][:], AF.Copy))
                        aw(); ao(s.activation(gbuf[it % 2][:, 512:1024], GB[1][:], AF.Copy))
                    if it >= 1:
                        b = it - 1
                        s.wait_ge(ct.h, s2_ct[b])
                        if b >= 1:
                            s.wait_ge(sgk.h, sgk0 + 128 * b)
                        elif g_ > 0:
                            s.wait_ge(sgk.h, sgk0)
                        aw(); ao(s.activation(xbuf[:, 0:512], XB[0][:], AF.Copy))
                        aw(); ao(s.activation(xbuf[:, 512:1024], XB[1][:], AF.Copy))
                @block.vector
                def _(v, it=it):
                    vo, vw = mkvo(cx, v)
                    if it <= 3:
                        v.wait_ge(ca.h, gev_end[it])
                        if it >= 2:
                            v.wait_ge(ct.h, s2_ct[it - 2])
                        elif g_ > 0:
                            v.wait_ge(ct.h, pe0g)
                        gb = gbuf[it % 2]
                        grv = ap(gb, 0, [(1024, 128), (256, 4), (1, 128)])
                        giv = ap(gb, 128, [(1024, 128), (256, 4), (1, 128)])
                        t1rb = ap(cx.cst["t1r"], 0, [(128, 128), (0, 4), (1, 128)])
                        t1ib = ap(cx.cst["t1i"], 0, [(128, 128), (0, 4), (1, 128)])
                        vw(); vo(v.tensor_tensor(tw1[:], grv, t1rb, ALU.mult))
                        vw(); vo(v.tensor_tensor(tw2[:], giv, t1ib, ALU.mult))
                        vw(); vo(v.tensor_tensor(grbuf[it % 2][:], tw1[:], tw2[:], ALU.subtract))
                        vw(); vo(v.tensor_tensor(tw1[:], grv, t1ib, ALU.mult))
                        vw(); vo(v.tensor_tensor(tw2[:], giv, t1rb, ALU.mult))
                        vw(); vo(v.tensor_tensor(gibuf[it % 2][:], tw1[:], tw2[:], ALU.add))
                @block.gpsimd
                def _(g, it=it):
                    if it >= 1:
                        b = it - 1
                        g.wait_ge(ca.h, xev_end[b])
                        for par in range(2):
                            for pr in range(4):
                                ch = g_ * 32 + b * 8 + 2 * pr + par
                                src = ap(xbuf, par * 64 * 1024 + pr * 128,
                                         [(1024, 64), (512, 2), (1, 128)])
                                dst = ap(D["dKLOC"], ch * 64 * 256,
                                         [(256, 64), (128, 2), (1, 128)])
                                g.dma_start(out=dst, in_=src).then_inc(sgk.h, 16)
                                sgk.val += 16
            for c5 in range(8 + (g_ * 24) // 32, 8 + ((g_ + 1) * 24) // 32):
                p5_chunk(cx, st5, c5)
    p5_close(cx, st5)
    cx.p2_done = True


def emit_p3(cx):
    nc, block, I, D = cx.nc, cx.block, cx.I, cx.D
    sd, sg, sa, cv, ca, ct = cx.sd, cx.sg, cx.sa, cx.cv, cx.ca, cx.ct
    PS = cx.PS

    p3 = contextlib.ExitStack()
    with p3:
        S3 = lambda nm, shp, dt=F32: p3.enter_context(nc.sbuf_tensor(nm, shp, dt))
        NB = 128
        m2b = [S3(f"p3_m2{j}", [128, 512], BF16) for j in range(2)]
        kb = [S3(f"p3_k{j}", [128, 1024], BF16) for j in range(2)]
        gbuf = S3("p3_gbuf", [128, 1024], BF16)
        grbuf = [S3(f"p3_grb{j}", [128, 512], BF16) for j in range(2)]
        gibuf = [S3(f"p3_gib{j}", [128, 512], BF16) for j in range(2)]
        xbuf = S3("p3_xbuf", [128, 1024], BF16)
        ybuf = S3("p3_ybuf", [128, 1024], BF16)
        hbuf = [S3(f"p3_hbuf{j}", [128, 1024], BF16) for j in range(2)]
        hrb = [S3(f"p3_hrb{j}", [128, 512], BF16) for j in range(2)]
        hib = [S3(f"p3_hib{j}", [128, 512], BF16) for j in range(2)]
        yob = S3("p3_yob", [64, 512], BF16)
        tw1 = S3("p3_tw1", [128, 512], BF16)
        tw2 = S3("p3_tw2", [128, 512], BF16)

        GB = (PS[:, 0:512], PS[:, 512:1024])
        XB = (PS[:, 1024:1536], PS[:, 1536:2048])
        HB = (PS[:, 2048:2560], PS[:, 2560:3072])
        YB = bass.AP(cx.PS, 3072, [[4096, 64], [1, 512]])

        sm2 = [cx.newsem(f"p3m2{j}") for j in range(2)]
        skb = [cx.newsem(f"p3kb{j}") for j in range(2)]
        syo = cx.newsem("p3yo")

        @block.vector
        def _(v):
            vo, vw = mkvo(cx, v)
            quiesce(cx, v)
            for j in range(2):
                vw(); vo(v.memset(m2b[j][:], 0.0))
        cv_z = cv.val

        ct0, ca0, cv0 = ct.val, ca.val, cv.val
        s1_ct = {}; s2_ct = {}; i1_ct = {}; i2_ct = {}
        gev_ca = {}; xev_ca = {}; hev_ca = {}; yev_ca = {}
        a_cv = {}; b_cv = {}; c_cv = {}
        vt, va, vv = ct0, ca0, cv0
        for it in range(NB + 2):
            if it < NB:       vt += 1; s1_ct[it] = vt
            if 1 <= it <= NB: vt += 1; s2_ct[it - 1] = vt
            if 1 <= it <= NB: vt += 1; i1_ct[it - 1] = vt
            if it >= 2:       vt += 1; i2_ct[it - 2] = vt
            if it < NB:       va += 2; gev_ca[it] = va
            if 1 <= it <= NB: va += 2; xev_ca[it - 1] = va
            if 1 <= it <= NB: va += 2; hev_ca[it - 1] = va
            if it >= 2:       va += 1; yev_ca[it - 2] = va
            if it < NB:       vv += 6; a_cv[it] = vv
            if 1 <= it <= NB: vv += 6; b_cv[it - 1] = vv
            if it >= 2:       vv += 6; c_cv[it - 2] = vv
        sm2_cnt = lambda b: 16 * (b // 2 + 1)
        skb_cnt = lambda b: 64 * (b // 2 + 1)
        syo_cnt = lambda b: 16 * (b + 1)
        sgq = cx.p0b_done["sg"]

        @block.sync
        def _(sync):
            sync.wait_ge(cv.h, cv_z)
            sync.wait_ge(sg.h, sgq)
            for b in range(NB):
                if b >= 2:
                    sync.wait_ge(ct.h, s1_ct[b - 2])
                src = ap(D["dB"], b * 8 * L, [(64, 64), (2 * L, 4), (L, 2), (1, 64)])
                sync.dma_start(out=ap(m2b[b % 2], 0, [(512, 64), (128, 4), (64, 2), (1, 64)]),
                               in_=src).then_inc(sm2[b % 2].h, 16)
                sm2[b % 2].val += 16

        def kb_load(g, bt):
            ch0 = bt * 8
            for ri in range(2):
                for par in range(2):
                    src = ap(D["dKLOC"], (ch0 + par) * 16384 + ri * 128,
                             [(256, 64), (2 * 16384, 4), (1, 128)])
                    dst = ap(kb[bt % 2], par * 64 * 1024 + ri * 512,
                             [(1024, 64), (128, 4), (1, 128)])
                    g.dma_start(out=dst, in_=src).then_inc(skb[bt % 2].h, 16)
                    skb[bt % 2].val += 16

        @block.gpsimd
        def _(g):
            g.wait_ge(cx.p2_sgk.h, cx.p2_sgk.val)
            kb_load(g, 0)
            kb_load(g, 1)

        for it in range(NB + 2):
            @block.tensor
            def _(t, it=it):
                if it < NB:
                    t.wait_ge(sm2[it % 2].h, sm2_cnt(it))
                    if it >= 1:
                        t.wait_ge(ca.h, gev_ca[it - 1])
                    ins = None
                    for pr in range(4):
                        og = GB[pr // 2][:, (pr % 2) * 256:(pr % 2) * 256 + 256]
                        ins = t.matmul(og, m2b[it % 2][:, pr * 128: pr * 128 + 128],
                                       cx.cst["cf"][:], start=(pr % 2 == 0), stop=(pr % 2 == 1))
                    ins.then_inc(ct.h, 1); ct.val += 1
                if 1 <= it <= NB:
                    bt = it - 1
                    t.wait_ge(cv.h, a_cv[bt])
                    if bt >= 1:
                        t.wait_ge(ca.h, xev_ca[bt - 1])
                    seq = [("s2wr", grbuf[bt % 2], 0, "start"), ("s2win", gibuf[bt % 2], 0, "stop"),
                           ("s2wi", grbuf[bt % 2], 1, "start"), ("s2wr", gibuf[bt % 2], 1, "stop")]
                    ins = None
                    for cname, srcb, xbank, role in seq:
                        for sl in range(2):
                            ins = t.matmul(XB[xbank][:, sl * 256:sl * 256 + 256],
                                           cx.cst[cname][:], srcb[:, sl * 256:sl * 256 + 256],
                                           start=(role == "start" and sl == 0),
                                           stop=(role == "stop" and sl == 1))
                    ins.then_inc(ct.h, 1); ct.val += 1
                if 1 <= it <= NB:
                    bt = it - 1
                    t.wait_ge(cv.h, b_cv[bt])
                    if bt >= 1:
                        t.wait_ge(ca.h, hev_ca[bt - 1])
                    ins = None
                    for pr in range(4):
                        og = HB[pr // 2][:, (pr % 2) * 256:(pr % 2) * 256 + 256]
                        t.matmul(og, ybuf[:, pr * 128:pr * 128 + 128], cx.cst["i1a"][:],
                                 start=(pr % 2 == 0), stop=False)
                        ins = t.matmul(og, ybuf[:, 512 + pr * 128:512 + pr * 128 + 128],
                                       cx.cst["i1b"][:], start=False, stop=(pr % 2 == 1))
                    ins.then_inc(ct.h, 1); ct.val += 1
                if it >= 2:
                    bt = it - 2
                    t.wait_ge(cv.h, c_cv[bt])
                    if bt >= 1:
                        t.wait_ge(ca.h, yev_ca[bt - 1])
                    t.matmul(YB, cx.cst["i2r"][:], hrb[bt % 2][:], start=True, stop=False)
                    ins = t.matmul(YB, cx.cst["i2in"][:], hib[bt % 2][:], start=False, stop=True)
                    ins.then_inc(ct.h, 1); ct.val += 1
            @block.scalar
            def _(s, it=it):
                ao, aw = mkao(cx, s)
                if it < NB:
                    s.wait_ge(ct.h, s1_ct[it])
                    if it >= 1:
                        s.wait_ge(cv.h, a_cv[it - 1])
                    aw(); ao(s.activation(gbuf[:, 0:512], GB[0][:], AF.Copy))
                    aw(); ao(s.activation(gbuf[:, 512:1024], GB[1][:], AF.Copy))
                if 1 <= it <= NB:
                    bt = it - 1
                    s.wait_ge(ct.h, s2_ct[bt])
                    if bt >= 1:
                        s.wait_ge(cv.h, b_cv[bt - 1])
                    aw(); ao(s.activation(xbuf[:, 0:512], XB[0][:], AF.Copy))
                    aw(); ao(s.activation(xbuf[:, 512:1024], XB[1][:], AF.Copy))
                if 1 <= it <= NB:
                    bt = it - 1
                    s.wait_ge(ct.h, i1_ct[bt])
                    if bt >= 2:
                        s.wait_ge(cv.h, c_cv[bt - 2])
                    aw(); ao(s.activation(hbuf[bt % 2][:, 0:512], HB[0][:], AF.Copy))
                    aw(); ao(s.activation(hbuf[bt % 2][:, 512:1024], HB[1][:], AF.Copy))
                if it >= 2:
                    bt = it - 2
                    s.wait_ge(ct.h, i2_ct[bt])
                    if bt >= 1:
                        s.wait_ge(syo.h, syo_cnt(bt - 1))
                    aw(); ao(s.activation(yob[:], YB, AF.Copy))
            @block.vector
            def _(v, it=it):
                vo, vw = mkvo(cx, v)
                if it < NB:
                    v.wait_ge(ca.h, gev_ca[it])
                    if it >= 2:
                        v.wait_ge(ct.h, s2_ct[it - 2])
                    grv = ap(gbuf, 0, [(1024, 128), (256, 4), (1, 128)])
                    giv = ap(gbuf, 128, [(1024, 128), (256, 4), (1, 128)])
                    t1rb = ap(cx.cst["t1r"], 0, [(128, 128), (0, 4), (1, 128)])
                    t1ib = ap(cx.cst["t1i"], 0, [(128, 128), (0, 4), (1, 128)])
                    vw(); vo(v.tensor_tensor(tw1[:], grv, t1rb, ALU.mult))
                    vw(); vo(v.tensor_tensor(tw2[:], giv, t1ib, ALU.mult))
                    vw(); vo(v.tensor_tensor(grbuf[it % 2][:], tw1[:], tw2[:], ALU.subtract))
                    vw(); vo(v.tensor_tensor(tw1[:], grv, t1ib, ALU.mult))
                    vw(); vo(v.tensor_tensor(tw2[:], giv, t1rb, ALU.mult))
                    vw(); vo(v.tensor_tensor(gibuf[it % 2][:], tw1[:], tw2[:], ALU.add))
                if 1 <= it <= NB:
                    bt = it - 1
                    v.wait_ge(ca.h, xev_ca[bt])
                    v.wait_ge(skb[bt % 2].h, skb_cnt(bt))
                    if bt >= 1:
                        v.wait_ge(ct.h, i1_ct[bt - 1])
                    xr = xbuf[:, 0:512]; xi = xbuf[:, 512:1024]
                    kr = kb[bt % 2][:, 0:512]; ki = kb[bt % 2][:, 512:1024]
                    vw(); vo(v.tensor_tensor(tw1[:], xr, kr, ALU.mult))
                    vw(); vo(v.tensor_tensor(tw2[:], xi, ki, ALU.mult))
                    vw(); vo(v.tensor_tensor(ybuf[:, 0:512], tw1[:], tw2[:], ALU.subtract))
                    vw(); vo(v.tensor_tensor(tw1[:], xr, ki, ALU.mult))
                    vw(); vo(v.tensor_tensor(tw2[:], xi, kr, ALU.mult))
                    vw(); vo(v.tensor_tensor(ybuf[:, 512:1024], tw1[:], tw2[:], ALU.add))
                if it >= 2:
                    bt = it - 2
                    v.wait_ge(ca.h, hev_ca[bt])
                    if bt >= 2:
                        v.wait_ge(ct.h, i2_ct[bt - 2])
                    hrv = ap(hbuf[bt % 2], 0, [(1024, 128), (256, 4), (128, 2), (1, 64)])
                    hiv = ap(hbuf[bt % 2], 64, [(1024, 128), (256, 4), (128, 2), (1, 64)])
                    t2rb = ap(cx.cst["t2r"], 0, [(128, 128), (0, 8), (1, 64)])
                    t2ib = ap(cx.cst["t2i"], 0, [(128, 128), (0, 8), (1, 64)])
                    vw(); vo(v.tensor_tensor(tw1[:], hrv, t2rb, ALU.mult))
                    vw(); vo(v.tensor_tensor(tw2[:], hiv, t2ib, ALU.mult))
                    vw(); vo(v.tensor_tensor(hrb[bt % 2][:], tw1[:], tw2[:], ALU.subtract))
                    vw(); vo(v.tensor_tensor(tw1[:], hrv, t2ib, ALU.mult))
                    vw(); vo(v.tensor_tensor(tw2[:], hiv, t2rb, ALU.mult))
                    vw(); vo(v.tensor_tensor(hib[bt % 2][:], tw1[:], tw2[:], ALU.add))
            @block.gpsimd
            def _(g, it=it):
                if it >= 1 and it + 1 <= NB - 1:
                    g.wait_ge(cv.h, b_cv[it - 1])
                    kb_load(g, it + 1)
                if it >= 2:
                    bt = it - 2
                    g.wait_ge(ca.h, yev_ca[bt])
                    dst = ap(D["dY"], bt * 8 * L, [(64, 64), (2 * L, 4), (L, 2), (1, 64)])
                    src = ap(yob, 0, [(512, 64), (128, 4), (64, 2), (1, 64)])
                    g.dma_start(out=dst, in_=src).then_inc(syo.h, 16)
                    syo.val += 16
        assert ct.val == vt
        assert ca.val == va
        assert cv.val == vv
    cx.p3_done = True


def emit_p4(cx):
    nc, block, I, D = cx.nc, cx.block, cx.I, cx.D
    sd, sg, sa, cv, ca, ct = cx.sd, cx.sg, cx.sa, cx.cv, cx.ca, cx.ct
    PS = cx.PS
    snap = [(x, x.val) for x in cx.all_sems]

    p4 = contextlib.ExitStack()
    with p4:
        S4 = lambda nm, shp, dt=F32: p4.enter_context(nc.sbuf_tensor(nm, shp, dt))
        wo = S4("p4_wo", [128, 16384], BF16)
        ybf = [S4(f"p4_y{j}", [128, 4096], BF16) for j in range(2)]
        ygb = S4("p4_yg", [128, 4096], BF16)
        u = S4("p4_u", [128, 512], BF16); s2b = S4("p4_s2", [128, 512], BF16)
        a1 = S4("p4_a1", [128, 512], BF16); th = S4("p4_th", [128, 512], BF16)
        zb = S4("p4_z", [128, 512], BF16); h1b = S4("p4_h1", [128, 512], BF16)
        sgv = S4("p4_sg", [128, 512])
        atile = S4("p4_at", [128, 512], BF16)

        swo = cx.newsem("p4wo")
        syl = [cx.newsem(f"p4y{j}") for j in range(2)]
        sap = cx.newsem("p4ap")

        @block.gpsimd
        def _(g):
            quiesce(cx, g)
            src = ap(I["wout"], 0, [(2 * H, 128), (128 * 2 * H, 8), (1, 2 * H)])
            g.dma_start(out=ap(wo, 0, [(16384, 128), (2 * H, 8), (1, 2 * H)]),
                        in_=src).then_inc(swo.h, 16)
            swo.val += 16

        DV4 = 8 * 7 + 8
        DA4 = 8 + 8
        DPE4 = 16
        ct0 = ct.val
        ca0 = ca.val
        cv0 = cv.val

        @block.sync
        def _(sync):
            quiesce(cx, sync)
            for tb in range(8):
                if tb >= 2:
                    sync.wait_ge(cv.h, cv0 + (tb - 2) * DV4 + 8 * 7)
                src = ap(D["dY"], tb * 512, [(L, 128), (128 * L, 8), (1, 512)])
                dst = ap(ybf[tb % 2], 0, [(4096, 128), (512, 8), (1, 512)])
                sync.dma_start(out=dst, in_=src).then_inc(syl[tb % 2].h, 16)
                syl[tb % 2].val += 16

        for tb in range(8):
            v0 = cv0 + tb * DV4
            a0 = ca0 + tb * DA4
            pe0 = ct0 + tb * DPE4
            @block.vector
            def _(v):
                vo, vw = mkvo(cx, v)
                if tb == 0:
                    quiesce(cx, v, snap)
                v.wait_ge(syl[tb % 2].h, 16 * (tb // 2 + 1))
                if tb > 0:
                    v.wait_ge(ct.h, pe0)
                for hc in range(8):
                    xnsl = cx.xnF[:, hc * L + tb * 512: hc * L + tb * 512 + 512]
                    ysl = ybf[tb % 2][:, hc * 512:hc * 512 + 512]
                    dcol = cx.sm["d_t"][:, hc:hc + 1]
                    vw(); vo(v.scalar_tensor_tensor(u[:], xnsl, dcol, ysl, ALU.mult, ALU.add))
                    vw(); vo(v.tensor_tensor(s2b[:], u[:], u[:], ALU.mult))
                    vw(); vo(v.tensor_scalar(s2b[:], s2b[:], GELU_A, 1.0, ALU.mult, ALU.add))
                    vw(); vo(v.tensor_tensor(a1[:], u[:], s2b[:], ALU.mult))
                    v.wait_ge(ca.h, a0 + hc + 1)
                    vw(); vo(v.tensor_tensor(zb[:], u[:], th[:], ALU.mult))
                    vw(); vo(v.tensor_scalar(h1b[:], u[:], 0.5, None, ALU.mult))
                    vw(); vo(v.scalar_tensor_tensor(ygb[:, hc * 512:hc * 512 + 512],
                                                    zb[:], 0.5, h1b[:], ALU.mult, ALU.add))
            @block.scalar
            def _(s):
                ao, aw = mkao(cx, s)
                if tb == 0:
                    quiesce(cx, s, snap)
                for hc in range(8):
                    s.wait_ge(cv.h, v0 + hc * 7 + 4)
                    if hc > 0:
                        s.wait_ge(cv.h, v0 + (hc - 1) * 7 + 5)
                    elif tb > 0:
                        s.wait_ge(cv.h, v0 - DV4 + 7 * 7 + 5)
                    aw(); ao(s.activation(th[:], a1[:], AF.Tanh, scale=GELU_C))
            @block.tensor
            def _(t):
                t.wait_ge(swo.h, 16)
                t.wait_ge(cv.h, v0 + 8 * 7)
                for pair in range(8):
                    for half in range(2):
                        oc = pair + 8 * half
                        bank = PS[:, ((2 * pair + half) % 4) * 512:((2 * pair + half) % 4) * 512 + 512]
                        if tb > 0 or pair >= 2:
                            prev = (tb * 8 + pair - 2)
                            t.wait_ge(ca.h, ca0 + (prev // 8) * DA4 + 8 + (prev % 8) + 1)
                            t.wait_ge(cv.h, cv0 + (prev // 8) * DV4 + 56 + (prev % 8) + 1)
                        ins = None
                        for hc in range(8):
                            ins = t.matmul(bank, wo[:, hc * 2048 + oc * 128: hc * 2048 + oc * 128 + 128],
                                           ygb[:, hc * 512:hc * 512 + 512],
                                           start=(hc == 0), stop=(hc == 7))
                        ins.then_inc(ct.h, 1)
                        ct.val += 1
            @block.scalar
            def _(s):
                ao, aw = mkao(cx, s)
                for pair in range(8):
                    s.wait_ge(ct.h, pe0 + 2 * pair + 2)
                    if pair > 0 or tb > 0:
                        prevp = tb * 8 + pair - 1
                        s.wait_ge(cv.h, cv0 + (prevp // 8) * DV4 + 56 + (prevp % 8) + 1)
                    gb = PS[:, ((2 * pair + 1) % 4) * 512:((2 * pair + 1) % 4) * 512 + 512]
                    aw(); ao(s.activation(sgv[:], gb, AF.Sigmoid,
                                          bias=cx.sm["bout_t"][:, pair + 8:pair + 9], scale=1.0))
            @block.vector
            def _(v):
                vo, vw = mkvo(cx, v)
                for pair in range(8):
                    v.wait_ge(ct.h, pe0 + 2 * pair + 1)
                    v.wait_ge(ca.h, a0 + 8 + pair + 1)
                    if pair > 0 or tb > 0:
                        prevp = tb * 8 + pair - 1
                        v.wait_ge(sap.h, 16 * (prevp + 1))
                    ab = PS[:, ((2 * pair) % 4) * 512:((2 * pair) % 4) * 512 + 512]
                    vw(); vo(v.scalar_tensor_tensor(atile[:], ab, cx.sm["bout_t"][:, pair:pair + 1],
                                                    sgv[:], ALU.add, ALU.mult))
            @block.gpsimd
            def _(g):
                for pair in range(8):
                    g.wait_ge(cv.h, v0 + 56 + pair + 1)
                    dst = ap(D["dP"], (pair * 128) * L + tb * 512, [(L, 128), (1, 512)])
                    g.dma_start(out=dst, in_=atile[:]).then_inc(sap.h, 16)
                    sap.val += 16
    cx.p4_done = True


def p5_setup(cx):
    nc, block, I, D = cx.nc, cx.block, cx.I, cx.D
    es = contextlib.ExitStack()
    S5 = lambda nm, shp, dt=F32: es.enter_context(nc.sbuf_tensor(nm, shp, dt))
    st = {"es": es}
    st["w1b"] = [S5(f"p5_w1{j}", [128, 4096], BF16) for j in range(2)]
    st["w3b"] = [S5(f"p5_w3{j}", [128, 4096], BF16) for j in range(2)]
    st["s1b"] = [S5(f"p5_s1{j}", [128, 512], BF16) for j in range(2)]
    st["h3b"] = [S5(f"p5_h3{j}", [128, 512], BF16) for j in range(2)]
    st["sw"] = [cx.newsem(f"p5w{j}") for j in range(2)]
    st["sht"] = [cx.newsem("p5ht0"), cx.newsem("p5ht1")]
    st["ct5"] = cx.newsem("ct5")
    st["ca5"] = cx.newsem("ca5")
    st["swcnt"] = {}
    st["snap"] = [(x, x.val) for x in cx.all_sems]

    def wload(g, u):
        sw = st["sw"][u % 2]
        if u < 16:
            for (w, t) in ((I["we1"], st["w1b"][u % 2]), (I["we3"], st["w3b"][u % 2])):
                src = ap(w, u * H * INTER, [(INTER, 128), (128 * INTER, 8), (1, INTER)])
                g.dma_start(out=ap(t, 0, [(4096, 128), (512, 8), (1, 512)]),
                            in_=src).then_inc(sw.h, 16)
                sw.val += 16
        else:
            hf = u - 16
            for (w, t) in ((I["ws1"], st["w1b"][u % 2]), (I["ws3"], st["w3b"][u % 2])):
                src = ap(w, hf * 512, [(H, 128), (128 * H, 8), (1, 512)])
                g.dma_start(out=ap(t, 0, [(4096, 128), (512, 8), (1, 512)]),
                            in_=src).then_inc(sw.h, 16)
                sw.val += 16
        st["swcnt"][u] = sw.val
    st["wload"] = wload

    @block.gpsimd
    def _(g):
        quiesce(cx, g)
        wload(g, 0)
        wload(g, 1)
    return st


def p5_chunk(cx, st, ci):
    block, PS = cx.block, cx.PS
    ct5, ca5 = st["ct5"], st["ca5"]
    sw, sht = st["sw"], st["sht"]
    w1b, w3b = st["w1b"], st["w3b"]
    s1b, h3b = st["s1b"], st["h3b"]
    lo, hi = 18 * ci, 18 * ci + 18

    @block.tensor
    def _(t):
        for n in range(lo, hi):
            u, k = n // 32, n % 32
            ic, ts = k // 8, k % 8
            if n == 0:
                quiesce(cx, t, st["snap"])
            if k == 0:
                t.wait_ge(sw[u % 2].h, st["swcnt"][u])
            pair = n % 2
            b1 = PS[:, (4 + 2 * pair) * 512:(4 + 2 * pair) * 512 + 512]
            b3 = PS[:, (5 + 2 * pair) * 512:(5 + 2 * pair) * 512 + 512]
            if n >= 2:
                t.wait_ge(ca5.h, 2 * (n - 2) + 2)
            ins = None
            for dc in range(8):
                ins = t.matmul(b1, w1b[u % 2][:, dc * 512 + ic * 128: dc * 512 + ic * 128 + 128],
                               cx.xnF[:, dc * L + ts * 512: dc * L + ts * 512 + 512],
                               start=(dc == 0), stop=(dc == 7))
            ins.then_inc(ct5.h, 1); ct5.val += 1
            for dc in range(8):
                ins = t.matmul(b3, w3b[u % 2][:, dc * 512 + ic * 128: dc * 512 + ic * 128 + 128],
                               cx.xnF[:, dc * L + ts * 512: dc * L + ts * 512 + 512],
                               start=(dc == 0), stop=(dc == 7))
            ins.then_inc(ct5.h, 1); ct5.val += 1

    @block.scalar
    def _(s):
        for n in range(lo, hi):
            u = n // 32
            ic = (n % 32) // 8
            pair = n % 2
            b1 = PS[:, (4 + 2 * pair) * 512:(4 + 2 * pair) * 512 + 512]
            s.wait_ge(ct5.h, 2 * n + 1)
            if SELF_WAITS and ca5.val > 0:
                s.wait_ge(ca5.h, ca5.val)
            if n >= 2:
                s.wait_ge(sht[n % 2].h, 32 * (n // 2))
            bcol = (cx.sm["be1_t"][:, u * 4 + ic:u * 4 + ic + 1] if u < 16
                    else cx.sm["bs1_t"][:, (u - 16) * 4 + ic:(u - 16) * 4 + ic + 1])
            ins = s.activation(s1b[n % 2][:], b1, AF.Silu, bias=bcol, scale=1.0)
            ins.then_inc(ca5.h, 1); ca5.val += 1
            b3 = PS[:, (5 + 2 * pair) * 512:(5 + 2 * pair) * 512 + 512]
            s.wait_ge(ct5.h, 2 * n + 2)
            ins = s.activation(h3b[n % 2][:], b3, AF.Copy)
            ins.then_inc(ca5.h, 1); ca5.val += 1

    @block.gpsimd
    def _(g):
        for n in range(lo, hi):
            u, k = n // 32, n % 32
            ic, ts = k // 8, k % 8
            g.wait_ge(ca5.h, 2 * n + 2)
            irow = (128 * 64 + (u - 16) * 512 + ic * 128) if u >= 16 else (u * 512 + ic * 128)
            dst = ap(cx.D["dS"], irow * L + ts * 512, [(L, 128), (1, 512)])
            g.dma_start(out=dst, in_=s1b[n % 2][:]).then_inc(sht[n % 2].h, 16)
            sht[n % 2].val += 16
            dst = ap(cx.D["dH"], irow * L + ts * 512, [(L, 128), (1, 512)])
            g.dma_start(out=dst, in_=h3b[n % 2][:]).then_inc(sht[n % 2].h, 16)
            sht[n % 2].val += 16
            if k == 31 and u + 2 < 18:
                g.wait_ge(ct5.h, 64 * (u + 1))
                st["wload"](g, u + 2)


def p5_close(cx, st):
    assert st["ct5"].val == 2 * 576
    assert st["ca5"].val == 2 * 576
    st["es"].close()
    cx.p5_done = True


def emit_p6(cx):
    nc, block, I, D = cx.nc, cx.block, cx.I, cx.D
    sd, sg, sa, cv, ca, ct = cx.sd, cx.sg, cx.sa, cx.cv, cx.ca, cx.ct
    PS = cx.PS
    snap = [(x, x.val) for x in cx.all_sems]
    cx.xnF_cm.__exit__(None, None, None)
    cx.xnF_freed = True

    p6 = contextlib.ExitStack()
    with p6:
        S6 = lambda nm, shp, dt=F32: p6.enter_context(nc.sbuf_tensor(nm, shp, dt))
        w2a = [S6(f"p6_w2{e}", [128, 4096], BF16) for e in range(16)]
        ws2b = S6("p6_ws2", [128, 8192], BF16)
        bep = S6("p6_bep", [128, 1024], BF16)
        hstr = [S6(f"p6_h{j}", [128, 2048], BF16) for j in range(4)]
        sstr = [S6(f"p6_s{j}", [128, 2048], BF16) for j in range(4)]
        abuf = S6("p6_a", [128, 2048], BF16)
        cmb6 = S6("p6_cmb", [128, 4096], BF16)
        mtile = [S6("p6_m", [128, 2048], BF16)] * 2

        sw6 = cx.newsem("p6w")
        sal = cx.newsem("p6a")
        smo = cx.newsem("p6mo")
        scm6 = cx.newsem("p6cm")

        @block.vector
        def _(v):
            vo, vw = mkvo(cx, v)
            quiesce(cx, v)
            vw(); vo(v.memset(bep[:], 0.0))
        cv_bz = cv.val

        @block.gpsimd
        def _(g):
            quiesce(cx, g)
            g.wait_ge(cv.h, cv_bz)
            g.dma_start(out=bep[0:16, :], in_=I["be2"].ap()).then_inc(sw6.h, 16)
            sw6.val += 16
            for e in range(16):
                g.wait_ge(sw6.h, sw6.val)
                src = ap(I["we2"], e * INTER * H, [(H, 128), (128 * H, 4), (1, H)])
                g.dma_start(out=ap(w2a[e], 0, [(4096, 128), (H, 4), (1, H)]),
                            in_=src).then_inc(sw6.h, 16)
                sw6.val += 16
            g.wait_ge(sw6.h, sw6.val)
            src = ap(I["ws2"], 0, [(H, 128), (128 * H, 8), (1, H)])
            g.dma_start(out=ap(ws2b, 0, [(8192, 128), (H, 8), (1, H)]),
                        in_=src).then_inc(sw6.h, 16)
            sw6.val += 16

        NTS = 16
        ct0 = ct.val
        ca0 = ca.val
        cv0 = cv.val

        VV6 = 16 * 5 + 8 + 8
        mm_ct = lambda ts, e: ct0 + ts * 17 + e + 1
        prod_cv = lambda ts, e: cv0 + ts * VV6 + (5 * (e + 1) if e < 16 else 88)
        evac_cv = lambda ts: cv0 + ts * VV6 + VV6

        she = [cx.newsem(f"p6he{j}") for j in range(4)]
        CNT4 = [5, 4, 4, 4]

        def she_cnt(ts, e):
            return 32 * (ts * CNT4[e % 4] + e // 4 + 1)

        def she_load(g, ts):
            for e in range(17):
                if e >= 4:
                    tsP, eP = ts, e - 4
                elif ts > 0:
                    tsP, eP = ts - 1, (16 if e % 4 == 0 else 12 + e % 4)
                else:
                    tsP = None
                if tsP is not None:
                    g.wait_ge(ct.h, mm_ct(tsP, eP))
                    g.wait_ge(she[e % 4].h, she_cnt(ts, e) - 32)
                nic = 8 if e == 16 else 4
                irow = 128 * 64 if e == 16 else e * 512
                for dn, buf in (("dS", sstr[e % 4]), ("dH", hstr[e % 4])):
                    src = ap(D[dn], irow * L + ts * 256, [(L, 128), (128 * L, nic), (1, 256)])
                    dst = ap(buf, 0, [(2048, 128), (256, nic), (1, 256)])
                    g.dma_start(out=dst, in_=src).then_inc(she[e % 4].h, 16)
                    she[e % 4].val += 16

        def cm6load(g, ts):
            if ts >= 1:
                g.wait_ge(cv.h, prod_cv(ts - 1, 15))
            src = ap(D["dCMBT"], ts * 256, [(0, 128), (L, 16), (1, 256)])
            g.dma_start(out=ap(cmb6, 0, [(4096, 128), (256, 16), (1, 256)]),
                        in_=src).then_inc(scm6.h, 16)
            scm6.val += 16

        def abuf_load(g, ts):
            if ts >= 1:
                g.wait_ge(cv.h, evac_cv(ts - 1))
            src = ap(D["dP"], ts * 256, [(L, 128), (128 * L, 8), (1, 256)])
            dst = ap(abuf, 0, [(2048, 128), (256, 8), (1, 256)])
            g.dma_start(out=dst, in_=src).then_inc(sal.h, 16)
            sal.val += 16

        @block.sync
        def _(sync):
            quiesce(cx, sync, snap)
            abuf_load(sync, 0)
            cm6load(sync, 0)
            she_load(sync, 0)

        for ts in range(NTS):
            @block.tensor
            def _(t):
                if ts == 0:
                    quiesce(cx, t, snap)
                    t.wait_ge(cv.h, cv_bz)
                    t.wait_ge(sw6.h, 16)
                if ts > 1:
                    t.wait_ge(cv.h, evac_cv(ts - 2))
                boff = (ts % 2) * 2048
                for dc in range(8):
                    og = PS[:, boff + dc * 256:boff + dc * 256 + 256]
                    t.matmul(og, bep[:, dc * 128:dc * 128 + 128],
                             cx.cmbT[:, ts * 256:ts * 256 + 256],
                             start=(dc % 2 == 0), stop=False)
                for e in range(17):
                    if ts == 0:
                        t.wait_ge(sw6.h, 16 * (e + 2))
                    t.wait_ge(cv.h, prod_cv(ts, e))
                    nic = 8 if e == 16 else 4
                    buf = hstr[e % 4]
                    wsrc = ws2b if e == 16 else w2a[e]
                    ins = None
                    for ic in range(nic):
                        for dc in range(8):
                            og = PS[:, boff + dc * 256:boff + dc * 256 + 256]
                            ins = t.matmul(og, wsrc[:, ic * H + dc * 128: ic * H + dc * 128 + 128],
                                           buf[:, ic * 256:ic * 256 + 256],
                                           start=False,
                                           stop=(e == 16 and ic == nic - 1 and dc % 2 == 1))
                    ins.then_inc(ct.h, 1)
                    ct.val += 1
            @block.vector
            def _(v):
                vo, vw = mkvo(cx, v)
                for e in range(17):
                    v.wait_ge(she[e % 4].h, she_cnt(ts, e))
                    if e == 0:
                        v.wait_ge(scm6.h, 16 * (ts + 1))
                    nic = 8 if e == 16 else 4
                    for ic in range(nic):
                        bcol = (cx.sm["be3_t"][:, e * 4 + ic:e * 4 + ic + 1] if e < 16
                                else cx.sm["bs3_t"][:, ic:ic + 1])
                        sl = slice(ic * 256, ic * 256 + 256)
                        vw(); vo(v.scalar_tensor_tensor(hstr[e % 4][:, sl], hstr[e % 4][:, sl],
                                                        bcol[:, 0:1], sstr[e % 4][:, sl],
                                                        ALU.add, ALU.mult))
                    if e < 16:
                        hv = ap(hstr[e % 4], 0, [(2048, 128), (256, 4), (1, 256)])
                        cmv = ap(cmb6, e * 256, [(4096, 128), (0, 4), (1, 256)])
                        vw(); vo(v.tensor_tensor(hv, hv, cmv, ALU.mult))
                assert cv.val == prod_cv(ts, 16)
                v.wait_ge(ct.h, ct0 + (ts + 1) * 17)
                v.wait_ge(sal.h, 16 * (ts + 1))
                if ts > 0:
                    v.wait_ge(smo.h, 16 * ts)
                boff = (ts % 2) * 2048
                for dc in range(8):
                    og = PS[:, boff + dc * 256:boff + dc * 256 + 256]
                    vw(); vo(v.scalar_tensor_tensor(mtile[ts % 2][:, dc * 256:dc * 256 + 256], og,
                                                    cx.sm["bs2_t"][:, dc:dc + 1],
                                                    abuf[:, dc * 256:dc * 256 + 256],
                                                    ALU.add, ALU.add))
                assert cv.val == evac_cv(ts)
            @block.sync
            def _(sync):
                if ts + 1 < NTS:
                    cm6load(sync, ts + 1)
                    she_load(sync, ts + 1)
                    abuf_load(sync, ts + 1)
            @block.gpsimd
            def _(g):
                g.wait_ge(cv.h, evac_cv(ts))
                dst = ap(D["dM"], ts * 256, [(L, 128), (128 * L, 8), (1, 256)])
                src = ap(mtile[ts % 2], 0, [(2048, 128), (256, 8), (1, 256)])
                g.dma_start(out=dst, in_=src).then_inc(smo.h, 16)
                smo.val += 16
    cx.p6_done = True


def emit_p7(cx):
    nc, block, I, D = cx.nc, cx.block, cx.I, cx.D
    sd, sa, cv = cx.sd, cx.sa, cx.cv
    snap = [(x, x.val) for x in cx.all_sems]

    p7 = contextlib.ExitStack()
    with p7:
        S7 = lambda nm, shp, dt=F32: p7.enter_context(nc.sbuf_tensor(nm, shp, dt))
        xb = [S7(f"p7_x{j}", [128, H]) for j in range(2)]
        mb = [S7(f"p7_m{j}", [128, H], BF16) for j in range(2)]
        ob = [S7(f"p7_o{j}", [128, H]) for j in range(2)]
        sx = [cx.newsem(f"p7x{j}") for j in range(2)]
        smt = [cx.newsem(f"p7m{j}") for j in range(2)]
        sot = [cx.newsem("p7o0"), cx.newsem("p7o1")]

        cv0 = cv.val
        def store_out(sync, j):
            sync.wait_ge(cv.h, cv0 + j + 1)
            sync.dma_start(out=cx.out.ap()[j * 128:(j + 1) * 128, :],
                           in_=ob[j % 2][:]).then_inc(sot[j % 2].h, 16)
            sot[j % 2].val += 16

        @block.sync
        def _(sync):
            quiesce(cx, sync, snap)
            for i in range(32):
                if i >= 2:
                    store_out(sync, i - 2)
                sync.dma_start(out=xb[i % 2][:], in_=I["x"].ap()[i * 128:(i + 1) * 128, :]
                               ).then_inc(sx[i % 2].h, 16)
                sx[i % 2].val += 16
            store_out(sync, 30)
            store_out(sync, 31)
            sync.wait_ge(sot[0].h, sot[0].val)
            sync.wait_ge(sot[1].h, sot[1].val)

        @block.scalar
        def _(s):
            quiesce(cx, s, snap)
            for i in range(32):
                if i >= 2:
                    s.wait_ge(cv.h, cv0 + (i - 2) + 1)
                src = ap(D["dM"], i * 128, [(L, H), (1, 128)])
                s.dma_start(out=mb[i % 2][:], in_=src, transpose=True
                            ).then_inc(smt[i % 2].h, 16)
                smt[i % 2].val += 16

        @block.vector
        def _(v):
            vo, vw = mkvo(cx, v)
            for i in range(32):
                v.wait_ge(sx[i % 2].h, 16 * (i // 2 + 1))
                v.wait_ge(smt[i % 2].h, 16 * (i // 2 + 1))
                if i >= 2:
                    v.wait_ge(sot[i % 2].h, 16 * ((i - 2) // 2 + 1))
                vw(); vo(v.tensor_tensor(ob[i % 2][:], xb[i % 2][:], mb[i % 2][:], ALU.add))


    cx.p7_done = True


def emit_taps(cx):
    if not cx.taps:
        return
    block = cx.block
    sd, sg = cx.sd, cx.sg

    @block.sync
    def _(sync):
        quiesce(cx, sync)
        for nm, t in cx.taps.items():
            if nm.split("__")[0] not in cx.D:
                continue
            src = cx.D[nm.split("__")[0]]
            sync.dma_start(out=t.ap(), in_=src.ap()).then_inc(sd.h, 16)
            sd.val += 16


def emit_fin(cx):
    block = cx.block
    sd = cx.sd

    @block.sync
    def _(sync):
        quiesce(cx, sync)
        if DEBUG["stop_after"] is not None:
            pass


def shard_inputs(inputs):
    gi = lambda k: np.ascontiguousarray(np.asarray(inputs[k]))
    x = gi("L0_attn_x")
    consts = host_consts()
    in_maps = []
    for c in range(NCORE):
        m = {
            "x": np.ascontiguousarray(x[c]),
            "logdt_t": np.ascontiguousarray(gi("L0_attn_log_dt").reshape(1, H)),
            "arl_t": np.ascontiguousarray(gi("L0_attn_A_real_log").T),
            "aim_t": np.ascontiguousarray(gi("L0_attn_A_imag").T),
            "cr_t": np.ascontiguousarray(gi("L0_attn_C_real").T),
            "ci_t": np.ascontiguousarray(gi("L0_attn_C_imag").T),
            "d_t": np.ascontiguousarray(gi("L0_attn_D").reshape(8, 128).T),
            "wout": gi("L0_attn_W_out"),
            "bout_t": np.ascontiguousarray(gi("L0_attn_b_out").reshape(16, 128).T),
            "gw_t": np.ascontiguousarray(gi("L0_ffn_gate_w").T),
            "gb": np.ascontiguousarray(gi("L0_ffn_gate_b").reshape(1, E)),
            "we1": gi("L0_ffn_We1"), "we2": gi("L0_ffn_We2"), "we3": gi("L0_ffn_We3"),
            "be1_t": np.ascontiguousarray(
                gi("L0_ffn_be1").T.reshape(4, 128, E).transpose(1, 2, 0).reshape(128, E * 4)),
            "be3_t": np.ascontiguousarray(
                gi("L0_ffn_be3").T.reshape(4, 128, E).transpose(1, 2, 0).reshape(128, E * 4)),
            "be2": gi("L0_ffn_be2"),
            "ws1": gi("L0_ffn_ws1"), "ws2": gi("L0_ffn_ws2"), "ws3": gi("L0_ffn_ws3"),
            "bs1_t": np.ascontiguousarray(gi("L0_ffn_bs1").reshape(8, 128).T),
            "bs3_t": np.ascontiguousarray(gi("L0_ffn_bs3").reshape(8, 128).T),
            "bs2_t": np.ascontiguousarray(gi("L0_ffn_bs2").reshape(8, 128).T),
        }
        for k, v in consts.items():
            m["c_" + k] = v
        in_maps.append(m)
    return in_maps


def kernel(**inputs):
    in_maps = shard_inputs(inputs)
    nc = build_nc()
    res = run_bass_kernel_spmd(nc, in_maps, core_ids=list(range(NCORE)))
    return np.stack([np.asarray(r["out"], np.float32) for r in res.results], axis=0)
